# version 09
# baseline (speedup 1.0000x reference)
import numpy as np
import ml_dtypes
import concourse.bass as bass
import concourse.bacc as bacc
import concourse.mybir as mybir
from concourse.bass_utils import run_bass_kernel_spmd
from contextlib import ExitStack

F32 = mybir.dt.float32
BF16 = mybir.dt.bfloat16
I32 = mybir.dt.int32
AF = mybir.ActivationFunctionType
ALU = mybir.AluOpType
AX = mybir.AxisListType

D = 1024
NH = 16
HD = 64
NE = 8
IE = 512
EXP_SHIFT = 2.0
LN_EPS = 1e-5

ENGINES = ("tensor", "vector", "scalar", "gpsimd", "sync")
SEMS = {}
SEM_NAMES = ("pe", "dve", "act", "pool", "st",
             "ldw", "ldx0", "ldx1", "ldp0", "ldp1",
             "ldk0", "ldk1", "ldv0", "ldv1", "ldr0", "ldr1",
             "ldxo", "ldwe0", "ldwe1", "ldwo",
             "lw0", "lw1", "lw2", "lw3")


class Sched:
    def __init__(self):
        self.steps = {e: [] for e in ENGINES}
        self.cnt = {s: 0 for s in SEM_NAMES}

    def inc(self, name, n=1):
        self.cnt[name] += n
        return self.cnt[name]

    def now(self, name):
        return self.cnt[name]

    def emit(self, eng, fn):
        self.steps[eng].append(fn)

    def wait(self, eng, sem, val):
        if val and val > 0:
            self.emit(eng, lambda e, s=sem, v=val: e.wait_ge(SEMS[s], v))

    def barrier(self):
        vals = [(s, self.cnt[s]) for s in SEM_NAMES]
        for eng in ENGINES:
            for s, v in vals:
                self.wait(eng, s, v)


def slot_list(c, hp, i, CH):
    ckt = CH // 128
    qoffA = ckt * c + 4 * i
    qoffB = ckt * (7 - c) + 4 * i
    bbase = ckt * (c + 1)
    slots = []
    for d in range(4):
        slots.append((0, qoffA + d, d))
    for d in range(4):
        slots.append((1, bbase + qoffB + d, d))
    for kt in range(qoffA):
        slots.append((0, kt, -1))
    for kt in range(qoffB):
        slots.append((1, bbase + kt, -1))
    assert len(slots) == 7 * ckt + 8 * i + 8
    return slots


def build_nc(S=8192, debug=False, stop_after=99, skip_p1=False,
             p2_no_k=False, p2_no_v=False, p2_no_q=False, p2_no_w=False):
    CH = S // 8
    T = 2 * CH
    PFX = 9 * CH
    NKT = PFX // 128
    NQT = T // 128
    NQB = CH // 512
    NR = 8 * NQB
    NCH1 = 2 * S // 512
    NCH2 = PFX // 512
    CHSH = CH.bit_length() - 1
    NSLOT = [7 * (CH // 128) + 8 * i + 8 for i in range(NQB)]
    NSLOT_TOT = 8 * sum(NSLOT)

    nc = bacc.Bacc("TRN2", target_bir_lowering=False,
                   dynamic_dma_scratch_size=32768)

    x_fm = nc.dram_tensor("x_fm", [D, 2 * S], F32, kind="ExternalInput")
    x_pfx = nc.dram_tensor("x_pfx", [D, PFX], F32, kind="ExternalInput")
    x_own = nc.dram_tensor("x_own", [T, D], F32, kind="ExternalInput")
    wq = nc.dram_tensor("wq", [D, D], F32, kind="ExternalInput")
    wk = nc.dram_tensor("wk", [D, D], F32, kind="ExternalInput")
    wv = nc.dram_tensor("wv", [D, NH * 65], F32, kind="ExternalInput")
    wo = nc.dram_tensor("wo", [D, D], F32, kind="ExternalInput")
    bo = nc.dram_tensor("bo", [1, D], F32, kind="ExternalInput")
    gwT = nc.dram_tensor("gwT", [D, NE], F32, kind="ExternalInput")
    lb = nc.dram_tensor("lb", [1, NE], F32, kind="ExternalInput")
    egwT = nc.dram_tensor("egwT", [NE, D, IE], F32, kind="ExternalInput")
    euwT = nc.dram_tensor("euwT", [NE, D, IE], F32, kind="ExternalInput")
    edwT = nc.dram_tensor("edwT", [NE, IE, D], F32, kind="ExternalInput")
    sgwT = nc.dram_tensor("sgwT", [D, 2 * IE], F32, kind="ExternalInput")
    suwT = nc.dram_tensor("suwT", [D, 2 * IE], F32, kind="ExternalInput")
    sdwT = nc.dram_tensor("sdwT", [2 * IE, D], F32, kind="ExternalInput")
    tbl = nc.dram_tensor("tbl", [1, NSLOT_TOT // 4], I32, kind="ExternalInput")
    y = nc.dram_tensor("y", [T, D], F32, kind="ExternalOutput")

    kind_sp = "ExternalOutput" if debug else "Internal"
    k_spill = nc.dram_tensor("k_spill", [8, 128, PFX], BF16, kind=kind_sp)
    v_spill = nc.dram_tensor("v_spill", [4, 128, NKT * 260], BF16, kind=kind_sp)
    if debug:
        dbg_ctx = nc.dram_tensor("dbg_ctx", [128, NQT * D], BF16,
                                 kind="ExternalOutput")
        dbg_xn = nc.dram_tensor("dbg_xn", [128, 8 * T], BF16,
                                kind="ExternalOutput")
        dbg_sc = nc.dram_tensor("dbg_sc", [1, 16], F32, kind="ExternalOutput")
        dbg_xnp = nc.dram_tensor("dbg_xnp", [128, 2 * 8 * 512], BF16,
                                 kind="ExternalOutput")
        dbg_srow = nc.dram_tensor("dbg_srow", [1, 3 * 512], F32,
                                  kind="ExternalOutput")

    sch = Sched()
    es_root = ExitStack()

    nc._addrmap = []

    def sb(es, name, shape, dtype):
        h = es.enter_context(nc.sbuf_tensor(name, shape, dtype))
        try:
            nbytes = shape[-1] * (2 if dtype == BF16 else 4)
            nc._addrmap.append((nc.lookup_mloc(h).addr, nbytes, name))
        except Exception:
            pass
        return h

    def ps(es, name, shape, dtype=F32):
        return es.enter_context(nc.psum_tensor(name, shape, dtype))

    tri = sb(es_root, "tri", [128, 128], BF16)
    ident = sb(es_root, "ident", [128, 128], BF16)
    id1 = sb(es_root, "id1", [1, 1], F32)
    iotaj = sb(es_root, "iotaj", [128, 128], BF16)
    iotai = sb(es_root, "iotai", [128, 1], F32)
    ones_col_bf = sb(es_root, "ones_col_bf", [128, 1], BF16)
    ones_col_f = sb(es_root, "ones_col_f", [128, 1], F32)
    ones_row_f = sb(es_root, "ones_row_f", [1, 128], F32)
    id65 = sb(es_root, "id65", [65, 65], F32)
    ones_row_bf = sb(es_root, "ones_row_bf", [1, 128], BF16)
    eps_col = sb(es_root, "eps_col", [128, 1], F32)
    nshift_col = sb(es_root, "nshift_col", [128, 1], F32)
    lb_sb = sb(es_root, "lb_sb", [1, NE], F32)
    gw_sb = sb(es_root, "gw_sb", [128, 8 * NE], BF16)
    csb = sb(es_root, "csb", [128, NE], F32)
    lbb = sb(es_root, "lbb", [128, NE], F32)
    cs_row = sb(es_root, "cs_row", [1, NE], F32)
    mm_sb = sb(es_root, "mm_sb", [128, NQT * NE], F32)
    sc_row = sb(es_root, "sc_row", [1, 16], F32)
    wmean_row = sb(es_root, "wmean_row", [1, NE], F32)
    xn_own = sb(es_root, "xn_own", [128, 8 * T], BF16)
    ctx_tm = sb(es_root, "ctx_tm", [128, NQT * D], BF16)

    es_attn = ExitStack()
    q_sb = sb(es_attn, "q_sb", [128, 8 * T], BF16)

    def p0(g):
        g.iota(iotaj[:, :], [[1, 128]], channel_multiplier=0,
               allow_small_or_imprecise_dtypes=True).then_inc(SEMS["pool"], 1)
        g.iota(iotai[:, :], [[1, 1]], channel_multiplier=1,
               allow_small_or_imprecise_dtypes=True).then_inc(SEMS["pool"], 1)
        g.memset(ones_col_bf[:, :], 1.0).then_inc(SEMS["pool"], 1)
        g.memset(ones_col_f[:, :], 1.0).then_inc(SEMS["pool"], 1)
        g.memset(ones_row_f[:, :], 1.0).then_inc(SEMS["pool"], 1)
        g.memset(ones_row_bf[:, :], 1.0).then_inc(SEMS["pool"], 1)
        g.memset(eps_col[:, :], LN_EPS).then_inc(SEMS["pool"], 1)
        g.memset(nshift_col[:, :], -EXP_SHIFT).then_inc(SEMS["pool"], 1)
        g.memset(id1[:, :], 1.0).then_inc(SEMS["pool"], 1)

    sch.emit("gpsimd", p0)
    t_pool0 = sch.inc("pool", 9)

    sch.wait("vector", "pool", t_pool0)
    sch.emit("vector", lambda v: v.tensor_scalar(
        tri[:, :], iotaj[:, :], iotai[:, :], None, ALU.is_ge)
        .then_inc(SEMS["dve"], 1))
    sch.inc("dve")
    sch.emit("vector", lambda v: v.tensor_scalar(
        ident[:, :], iotaj[:, :], iotai[:, :], None, ALU.is_equal)
        .then_inc(SEMS["dve"], 1))
    sch.inc("dve")
    sch.emit("vector", lambda v: v.tensor_scalar(
        id65[:, :], iotaj[0:65, 0:65], iotai[0:65, :], None, ALU.is_equal)
        .then_inc(SEMS["dve"], 1))
    t_consts = sch.inc("dve")

    def p0w(g):
        g.dma_start(out=lb_sb[:, :], in_=lb[:, :]).then_inc(SEMS["ldw"], 16)
        g.dma_start(out=gw_sb.ap().rearrange("p (c e) -> p c e", c=8),
                    in_=bass.AP(gwT, 0, [[NE, 128], [128 * NE, 8], [1, NE]])
                    ).then_inc(SEMS["ldw"], 16)

    sch.emit("gpsimd", p0w)
    t_smallw = sch.inc("ldw", 32)

    ps_p0 = ExitStack()
    psCs = ps(ps_p0, "psCs", [1, NE])
    psCb = ps(ps_p0, "psCb", [128, NE])
    psLb = ps(ps_p0, "psLb", [128, NE])

    sch.wait("tensor", "ldw", t_smallw)
    sch.wait("tensor", "pool", t_pool0)

    def csmm(t):
        for dc in range(8):
            ins = t.matmul(psCs[:, :], ones_col_bf[:, :],
                           gw_sb[:, dc * NE:(dc + 1) * NE],
                           start=(dc == 0), stop=(dc == 7))
        ins.then_inc(SEMS["pe"], 1)

    sch.emit("tensor", csmm)
    t_cs = sch.inc("pe")
    sch.wait("vector", "pe", t_cs)
    sch.emit("vector", lambda v: v.tensor_copy(cs_row[:, :], psCs[:, :])
             .then_inc(SEMS["dve"], 1))
    t_csr = sch.inc("dve")
    sch.wait("tensor", "dve", t_csr)

    def cbmm(t):
        t.matmul(psCb[:, :], ones_row_f[:, :], cs_row[:, :],
                 start=True, stop=True)
        t.matmul(psLb[:, :], ones_row_f[:, :], lb_sb[:, :],
                 start=True, stop=True).then_inc(SEMS["pe"], 1)

    sch.emit("tensor", cbmm)
    t_cbm = sch.inc("pe")
    sch.wait("vector", "pe", t_cbm)
    sch.emit("vector", lambda v: v.tensor_copy(csb[:, :], psCb[:, :])
             .then_inc(SEMS["dve"], 1))
    sch.inc("dve")
    sch.emit("vector", lambda v: v.tensor_copy(lbb[:, :], psLb[:, :])
             .then_inc(SEMS["dve"], 1))
    t_cblb = sch.inc("dve")
    ps_p0.close()

    es_p1, ps_p1 = ExitStack(), ExitStack()
    xch = [sb(es_p1, f"xch{i}", [128, 8 * 512], BF16) for i in range(2)]
    x2ch = sb(es_p1, "x2ch", [128, 8 * 512], BF16)
    xtmp = sb(es_p1, "xtmp", [128, 512], F32)
    srow = sb(es_p1, "srow", [1, 5 * 512], F32)
    rc_sb = sb(es_p1, "rc_sb", [128, 2], F32)
    lgt = sb(es_p1, "lgt", [128, NE], F32)
    sgt = sb(es_p1, "sgt", [128, NE], F32)
    r1 = sb(es_p1, "r1", [128, 1], F32)
    r2 = sb(es_p1, "r2", [128, 1], F32)
    m1 = sb(es_p1, "m1", [128, NE], F32)
    m2 = sb(es_p1, "m2", [128, NE], F32)
    s2t = sb(es_p1, "s2t", [128, NE], F32)
    den = sb(es_p1, "den", [128, 1], F32)
    w1c = sb(es_p1, "w1c", [128, 1], F32)
    w2c = sb(es_p1, "w2c", [128, 1], F32)
    wmt = sb(es_p1, "wmt", [128, 16], F32)
    psSum = ps(ps_p1, "psSum", [1, 512])
    psSq = ps(ps_p1, "psSq", [1, 512])
    psB1 = ps(ps_p1, "psB1", [128, 512])
    psB2 = ps(ps_p1, "psB2", [128, 512])
    psG = [ps(ps_p1, f"psG{i}", [128, NE]) for i in range(2)]
    psSC = ps(ps_p1, "psSC", [1, 16])
    psT2 = ps(ps_p1, "psT2", [128, 2])

    sch.wait("tensor", "dve", t_cblb)

    xc_pe = [0, 0]
    xc_dve = [0, 0]
    xc_act = [0, 0]
    xc_ld = [0, 0]
    x2_pe = 0
    rows_dve = 0
    rows_act = 0
    rows_pe = 0
    xn_dve_prev = 0
    chain_prev = 0
    psg_dve = [0, 0]
    rc_dve = 0
    rc_read = 0
    wmt_pe = 0

    for ci in (range(NCH1) if not skip_p1 else []):
        pb = ci % 2
        sch.wait("gpsimd", "pe", xc_pe[pb])
        sch.wait("gpsimd", "dve", xc_dve[pb])
        sch.wait("gpsimd", "act", xc_act[pb])

        def ldx(g, ci=ci, pb=pb):
            g.dma_start(out=xch[pb].ap().rearrange("p (c w) -> p c w", c=8),
                        in_=bass.AP(x_fm, ci * 512,
                                    [[2 * S, 128], [128 * 2 * S, 8], [1, 512]])
                        ).then_inc(SEMS["ldx%d" % pb], 16)

        sch.emit("gpsimd", ldx)
        xc_ld[pb] = sch.inc("ldx%d" % pb, 16)

        sch.wait("scalar", "ldx%d" % pb, xc_ld[pb])
        sch.wait("scalar", "pe", x2_pe)
        sch.emit("scalar", lambda s, pb=pb: s.activation(
            x2ch[:, :], xch[pb][:, :], AF.Square).then_inc(SEMS["act"], 1))
        t_sq = sch.inc("act")
        xc_act[pb] = t_sq

        sch.wait("tensor", "ldx%d" % pb, xc_ld[pb])
        sch.wait("tensor", "dve", rows_dve)

        def statmm(t, pb=pb):
            for dc in range(8):
                t.matmul(psSum[:, :], ones_col_bf[:, :],
                         xch[pb][:, dc * 512:(dc + 1) * 512],
                         start=(dc == 0), stop=(dc == 7))

        sch.emit("tensor", statmm)
        sch.wait("tensor", "act", t_sq)

        def sqmm(t):
            for dc in range(8):
                ins = t.matmul(psSq[:, :], ones_col_bf[:, :],
                               x2ch[:, dc * 512:(dc + 1) * 512],
                               start=(dc == 0), stop=(dc == 7))
            ins.then_inc(SEMS["pe"], 1)

        sch.emit("tensor", sqmm)
        t_stat = sch.inc("pe")
        x2_pe = t_stat
        xc_pe[pb] = t_stat

        sch.wait("vector", "pe", t_stat)
        sch.wait("vector", "pe", rows_pe)
        sch.wait("vector", "act", rows_act)

        def rows1(v):
            v.tensor_scalar(srow[:, 0:512], psSum[:, :], -1.0 / D, None,
                            ALU.mult).then_inc(SEMS["dve"], 1)
            v.tensor_scalar(srow[:, 512:1024], psSq[:, :], 1.0 / D, None,
                            ALU.mult).then_inc(SEMS["dve"], 1)

        sch.emit("vector", rows1)
        t_r1s = sch.inc("dve", 2)
        rows_dve = t_r1s
        sch.wait("vector", "dve", t_r1s)
        sch.emit("vector", lambda v: v.tensor_tensor(
            srow[:, 1024:1536], srow[:, 0:512], srow[:, 0:512], ALU.mult)
            .then_inc(SEMS["dve"], 1))
        t_m2r = sch.inc("dve")
        sch.wait("vector", "dve", t_m2r)
        sch.emit("vector", lambda v: v.tensor_tensor(
            srow[:, 1024:1536], srow[:, 512:1024], srow[:, 1024:1536],
            ALU.subtract).then_inc(SEMS["dve"], 1))
        t_var = sch.inc("dve")
        sch.wait("scalar", "dve", t_var)
        sch.emit("scalar", lambda s: s.activation(
            srow[:, 1536:2048], srow[:, 1024:1536], AF.Ln,
            bias=eps_col[0:1, :]).then_inc(SEMS["act"], 1))
        t_ln = sch.inc("act")
        sch.wait("scalar", "act", t_ln)
        sch.emit("scalar", lambda s: s.activation(
            srow[:, 2048:2560], srow[:, 1536:2048], AF.Exp, scale=-0.5)
            .then_inc(SEMS["act"], 1))
        t_rstd = sch.inc("act")
        rows_act = t_rstd
        sch.wait("vector", "act", t_rstd)
        sch.emit("vector", lambda v: v.tensor_tensor(
            srow[:, 0:512], srow[:, 0:512], srow[:, 2048:2560], ALU.mult)
            .then_inc(SEMS["dve"], 1))
        t_nmr = sch.inc("dve")
        rows_dve = t_nmr

        if ci < T // 512:
            sch.wait("tensor", "act", t_rstd)
            sch.wait("tensor", "dve", t_nmr)
            sch.wait("tensor", "dve", xn_dve_prev)

            def bcast(t):
                t.matmul(psB1[:, :], ones_row_f[:, :], srow[:, 2048:2560],
                         start=True, stop=True)
                t.matmul(psB2[:, :], ones_row_f[:, :], srow[:, 0:512],
                         start=True, stop=True).then_inc(SEMS["pe"], 1)

            sch.emit("tensor", bcast)
            t_bc = sch.inc("pe")
            rows_pe = max(rows_pe, t_bc)

            sch.wait("vector", "pe", t_bc)
            prev_b = 0
            for dc in range(8):
                sch.wait("vector", "dve", prev_b)
                sch.emit("vector", lambda v, pb=pb, dc=dc: v.tensor_tensor(
                    xtmp[:, :], xch[pb][:, dc * 512:(dc + 1) * 512],
                    psB1[:, :], ALU.mult).then_inc(SEMS["dve"], 1))
                t_a = sch.inc("dve")
                sch.wait("vector", "dve", t_a)
                sch.emit("vector", lambda v, ci=ci, dc=dc: v.tensor_tensor(
                    xn_own[:, dc * T + ci * 512: dc * T + ci * 512 + 512],
                    xtmp[:, :], psB2[:, :], ALU.add).then_inc(SEMS["dve"], 1))
                prev_b = sch.inc("dve")
            xn_dve_prev = sch.now("dve")
            xc_dve[pb] = xn_dve_prev

        for j in range(4):
            gb = j % 2
            tile = ci * 4 + j
            sch.wait("tensor", "dve", psg_dve[gb])

            def gmm(t, pb=pb, j=j, gb=gb):
                for dc in range(8):
                    ins = t.matmul(
                        psG[gb][:, :],
                        xch[pb][:, dc * 512 + j * 128: dc * 512 + j * 128 + 128],
                        gw_sb[:, dc * NE:(dc + 1) * NE],
                        start=(dc == 0), stop=(dc == 7))
                ins.then_inc(SEMS["pe"], 1)

            sch.emit("tensor", gmm)
            t_gmm = sch.inc("pe")
            xc_pe[pb] = t_gmm

            sch.wait("tensor", "act", t_rstd)
            sch.wait("tensor", "dve", t_nmr)
            sch.wait("tensor", "dve", rc_dve)

            def trr(t, j=j):
                t.transpose(psT2[:, 0:1],
                            srow[0:1, 2048 + j * 128: 2048 + j * 128 + 128],
                            id1[:, :])
                t.transpose(psT2[:, 1:2], srow[0:1, j * 128: j * 128 + 128],
                            id1[:, :]).then_inc(SEMS["pe"], 1)

            sch.emit("tensor", trr)
            t_tr = sch.inc("pe")
            rows_pe = max(rows_pe, t_tr)

            sch.wait("vector", "pe", t_tr)
            sch.wait("vector", "dve", rc_read)
            sch.emit("vector", lambda v: v.tensor_copy(rc_sb[:, :], psT2[:, :])
                     .then_inc(SEMS["dve"], 1))
            rc_dve = sch.inc("dve")

            sch.wait("vector", "pe", t_gmm)
            sch.wait("vector", "dve", rc_dve)
            sch.wait("vector", "dve", chain_prev)
            sch.emit("vector", lambda v, gb=gb: v.tensor_scalar(
                lgt[:, :], psG[gb][:, :], rc_sb[:, 0:1], None, ALU.mult)
                .then_inc(SEMS["dve"], 1))
            tk = sch.inc("dve")
            psg_dve[gb] = tk
            sch.wait("vector", "dve", tk)
            sch.emit("vector", lambda v: v.scalar_tensor_tensor(
                lgt[:, :], csb[:, :], rc_sb[:, 1:2], lgt[:, :], ALU.mult,
                ALU.add).then_inc(SEMS["dve"], 1))
            tk = sch.inc("dve")
            sch.wait("vector", "dve", tk)
            sch.emit("vector", lambda v: v.tensor_tensor(
                lgt[:, :], lgt[:, :], lbb[:, :], ALU.add)
                .then_inc(SEMS["dve"], 1))
            t_lg = sch.inc("dve")
            rc_read = t_lg

            sch.wait("scalar", "dve", t_lg)
            sch.wait("scalar", "dve", chain_prev)
            sch.emit("scalar", lambda s: s.activation(
                sgt[:, :], lgt[:, :], AF.Sigmoid).then_inc(SEMS["act"], 1))
            t_sig = sch.inc("act")

            def dve_op(fn):
                sch.emit("vector",
                         lambda v, fn=fn: fn(v).then_inc(SEMS["dve"], 1))
                return sch.inc("dve")

            sch.wait("vector", "act", t_sig)
            sch.wait("vector", "pe", wmt_pe)
            tk = dve_op(lambda v: v.reduce_max(r1[:, :], sgt[:, :], axis=AX.X))
            sch.wait("vector", "dve", tk)
            tk = dve_op(lambda v: v.tensor_scalar(m1[:, :], sgt[:, :],
                                                  r1[:, :], None, ALU.is_ge))
            sch.wait("vector", "dve", tk)
            tk = dve_op(lambda v: v.scalar_tensor_tensor(
                s2t[:, :], m1[:, :], -1e9, sgt[:, :], ALU.mult, ALU.add))
            sch.wait("vector", "dve", tk)
            tk = dve_op(lambda v: v.reduce_max(r2[:, :], s2t[:, :], axis=AX.X))
            sch.wait("vector", "dve", tk)
            tk = dve_op(lambda v: v.tensor_scalar(m2[:, :], s2t[:, :],
                                                  r2[:, :], None, ALU.is_ge))
            tk = dve_op(lambda v: v.tensor_tensor(den[:, :], r1[:, :],
                                                  r2[:, :], ALU.add))
            sch.wait("vector", "dve", tk)
            tk = dve_op(lambda v: v.tensor_scalar(den[:, :], den[:, :], 1e-8,
                                                  None, ALU.add))
            sch.wait("vector", "dve", tk)
            tk = dve_op(lambda v: v.reciprocal(den[:, :], den[:, :]))
            sch.wait("vector", "dve", tk)
            tk = dve_op(lambda v: v.tensor_tensor(w1c[:, :], r1[:, :],
                                                  den[:, :], ALU.mult))
            tk = dve_op(lambda v: v.tensor_tensor(w2c[:, :], r2[:, :],
                                                  den[:, :], ALU.mult))
            sch.wait("vector", "dve", tk)
            tk = dve_op(lambda v: v.tensor_scalar(wmt[:, 0:8], m1[:, :],
                                                  w1c[:, :], None, ALU.mult))
            sch.wait("vector", "dve", tk)
            tk = dve_op(lambda v: v.scalar_tensor_tensor(
                wmt[:, 0:8], m2[:, :], w2c[:, :], wmt[:, 0:8], ALU.mult,
                ALU.add))
            tk = dve_op(lambda v: v.tensor_tensor(wmt[:, 8:16], m1[:, :],
                                                  m2[:, :], ALU.add))
            if tile < NQT:
                sch.wait("vector", "dve", tk)
                tk = dve_op(lambda v, tile=tile: v.tensor_copy(
                    mm_sb[:, tile * NE:(tile + 1) * NE], wmt[:, 8:16]))
            chain_prev = tk

            sch.wait("tensor", "dve", chain_prev)
            sch.emit("tensor", lambda t, tile=tile: t.matmul(
                psSC[:, :], ones_col_f[:, :], wmt[:, :],
                start=(tile == 0), stop=(tile == NCH1 * 4 - 1),
                skip_group_check=True).then_inc(SEMS["pe"], 1))
            wmt_pe = sch.inc("pe")

    if not skip_p1:
        sch.wait("vector", "pe", wmt_pe)
        sch.emit("vector", lambda v: v.tensor_copy(sc_row[:, :], psSC[:, :])
                 .then_inc(SEMS["dve"], 1))
        tk = sch.inc("dve")
        sch.wait("vector", "dve", tk)
        sch.emit("vector", lambda v: v.tensor_scalar(
            sc_row[:, 8:16], sc_row[:, 8:16], 1.0, None, ALU.max)
            .then_inc(SEMS["dve"], 1))
        tk = sch.inc("dve")
        sch.wait("vector", "dve", tk)
        sch.emit("vector", lambda v: v.reciprocal(sc_row[:, 8:16],
                                                  sc_row[:, 8:16])
                 .then_inc(SEMS["dve"], 1))
        tk = sch.inc("dve")
        sch.wait("vector", "dve", tk)
        sch.emit("vector", lambda v: v.tensor_tensor(
            wmean_row[:, :], sc_row[:, 0:8], sc_row[:, 8:16], ALU.mult)
            .then_inc(SEMS["dve"], 1))
        t_wmean = sch.inc("dve")
    else:
        t_wmean = 0

    es_p1.close()
    ps_p1.close()
    sch.barrier()

    es_p2, ps_p2 = ExitStack(), ExitStack()
    wq_sb = sb(es_p2, "wq_sb", [128, 8 * D], BF16)
    wk_sb = sb(es_p2, "wk_sb", [128, 8 * D], BF16)
    wv_sb = sb(es_p2, "wv_sb", [128, 8 * NH * 65], BF16)
    xpch = [sb(es_p2, f"xpch{i}", [128, 8 * 512], BF16) for i in range(2)]
    x2q = [sb(es_p2, f"x2q{i}", [128, 512], BF16) for i in range(2)]
    xtp = sb(es_p2, "xtp", [128, 512], BF16)
    srow2 = sb(es_p2, "srow2", [1, 3 * 512], F32)
    xnp = [sb(es_p2, f"xnp{i}", [128, 8 * 512], BF16) for i in range(2)]
    kev = [sb(es_p2, f"kev{i}", [128, 512], BF16) for i in range(2)]
    vev = [sb(es_p2, f"vev{i}", [128, 260], BF16) for i in range(2)]
    psSum2 = ps(ps_p2, "psSum2", [1, 512])
    psSq2 = ps(ps_p2, "psSq2", [1, 512])
    psB12 = ps(ps_p2, "psB12", [128, 512])
    psB22 = ps(ps_p2, "psB22", [128, 512])
    psK = [ps(ps_p2, f"psK{i}", [128, 512]) for i in range(2)]
    psV = [ps(ps_p2, f"psV{i}", [128, 260]) for i in range(2)]

    def p2w(g):
        i = 0
        for dst, src, w in ((wq_sb, wq, D), (wk_sb, wk, D),
                            (wv_sb, wv, NH * 65)):
            for c in range(8):
                stream = i % 4
                if i >= 4:
                    g.wait_ge(SEMS["lw%d" % stream], 16 * (i // 4))
                g.dma_start(out=dst[:, c * w:(c + 1) * w],
                            in_=bass.AP(src, c * 128 * w, [[w, 128], [1, w]])
                            ).then_inc(SEMS["lw%d" % stream], 16)
                i += 1

    if not p2_no_w:
        sch.emit("gpsimd", p2w)
        for stream in range(4):
            sch.wait("tensor", "lw%d" % stream, sch.inc("lw%d" % stream, 96))

    xp_pe = [0, 0]
    xp_dve = [0, 0]
    xp_act = [0, 0]
    xp_ld = [0, 0]
    pend_kv = None

    def emit_kv(pb, ci, t_xn):
        for hp in (range(8) if not p2_no_k else []):
            kb = hp % 2
            sch.wait("tensor", "dve", t_xn)
            sch.wait("tensor", "act", kev_act[kb])

            def kmm(t, pb=pb, hp=hp, kb=kb):
                for dc in range(8):
                    ins = t.matmul(
                        psK[kb][:, :],
                        wk_sb[:, dc * D + hp * 128: dc * D + hp * 128 + 128],
                        xnp[pb][:, dc * 512:(dc + 1) * 512],
                        start=(dc == 0), stop=(dc == 7))
                ins.then_inc(SEMS["pe"], 1)

            sch.emit("tensor", kmm)
            t_kmm = sch.inc("pe")

            sch.wait("scalar", "pe", t_kmm)
            sch.wait("scalar", "ldk%d" % kb, kev_ld[kb])
            sch.emit("scalar", lambda s, kb=kb: s.copy(
                kev[kb][:, :], psK[kb][:, :]).then_inc(SEMS["act"], 1))
            kev_act[kb] = sch.inc("act")

            sch.wait("gpsimd", "act", kev_act[kb])

            def kspill(g, ci=ci, hp=hp, kb=kb):
                g.dma_start(
                    out=bass.AP(k_spill, hp * 128 * PFX + ci * 512,
                                [[PFX, 128], [1, 512]]),
                    in_=kev[kb][:, :]).then_inc(SEMS["ldk%d" % kb], 16)

            sch.emit("gpsimd", kspill)
            kev_ld[kb] = sch.inc("ldk%d" % kb, 16)

        for j in (range(4) if not p2_no_v else []):
            kt = ci * 4 + j
            for gi in range(4):
                vb = (j * 4 + gi) % 2
                sch.wait("tensor", "dve", vev_dve[vb])

                def vmm(t, pb=pb, j=j, gi=gi, vb=vb):
                    for dc in range(8):
                        ins = t.matmul(
                            psV[vb][:, :],
                            xnp[pb][:, dc * 512 + j * 128:
                                    dc * 512 + j * 128 + 128],
                            wv_sb[:, dc * (NH * 65) + gi * 260:
                                  dc * (NH * 65) + gi * 260 + 260],
                            start=(dc == 0), stop=(dc == 7))
                    ins.then_inc(SEMS["pe"], 1)

                sch.emit("tensor", vmm)
                t_vmm = sch.inc("pe")

                sch.wait("vector", "pe", t_vmm)
                sch.wait("vector", "ldv%d" % vb, vev_ld[vb])
                sch.emit("vector", lambda v, vb=vb: v.tensor_copy(
                    vev[vb][:, :], psV[vb][:, :]).then_inc(SEMS["dve"], 1))
                tk1 = sch.inc("dve")
                sch.wait("vector", "dve", tk1)
                sch.emit("vector", lambda v, vb=vb: v.memset(
                    bass.AP(vev[vb], 64, [[260, 128], [65, 4]]), 1.0)
                    .then_inc(SEMS["dve"], 1))
                vev_dve[vb] = sch.inc("dve")

                sch.wait("gpsimd", "dve", vev_dve[vb])

                def vspill(g, kt=kt, gi=gi, vb=vb):
                    g.dma_start(
                        out=bass.AP(v_spill,
                                    gi * 128 * NKT * 260 + kt * 260,
                                    [[NKT * 260, 128], [1, 260]]),
                        in_=vev[vb][:, :]).then_inc(SEMS["ldv%d" % vb], 16)

                sch.emit("gpsimd", vspill)
                vev_ld[vb] = sch.inc("ldv%d" % vb, 16)
        xnp_pe[pb] = sch.now("pe")
    x2q_pe = [0, 0]
    rows2_dve = 0
    rows2_act = 0
    rows2_pe = 0
    xnp_dve = [0, 0]
    xnp_pe = [0, 0]
    kev_act = [0, 0]
    kev_ld = [0, 0]
    vev_dve = [0, 0]
    vev_ld = [0, 0]

    for ci in range(NCH2):
        pb = ci % 2
        sch.wait("gpsimd", "pe", xp_pe[pb])
        sch.wait("gpsimd", "dve", xp_dve[pb])
        sch.wait("gpsimd", "act", xp_act[pb])

        def ldxp(g, ci=ci, pb=pb):
            g.dma_start(out=xpch[pb].ap().rearrange("p (c w) -> p c w", c=8),
                        in_=bass.AP(x_pfx, ci * 512,
                                    [[PFX, 128], [128 * PFX, 8], [1, 512]])
                        ).then_inc(SEMS["ldp%d" % pb], 16)

        sch.emit("gpsimd", ldxp)
        xp_ld[pb] = sch.inc("ldp%d" % pb, 16)

        t_sq = [0] * 8
        pe_base = sch.now("pe")
        for qh in range(8):
            qb2 = qh % 2
            if qh == 0:
                sch.wait("scalar", "ldp%d" % pb, xp_ld[pb])
            sch.wait("scalar", "pe",
                     x2q_pe[qb2] if qh < 2 else pe_base + qh - 1)
            sch.emit("scalar", lambda s, pb=pb, qh=qh, qb2=qb2: s.activation(
                x2q[qb2][:, :], xpch[pb][:, qh * 512:(qh + 1) * 512],
                AF.Square).then_inc(SEMS["act"], 1))
            t_sq[qh] = sch.inc("act")
        xp_act[pb] = t_sq[7]

        sch.wait("tensor", "ldp%d" % pb, xp_ld[pb])
        sch.wait("tensor", "dve", rows2_dve)

        def statmm2(t, pb=pb):
            for dc in range(8):
                t.matmul(psSum2[:, :], ones_col_bf[:, :],
                         xpch[pb][:, dc * 512:(dc + 1) * 512],
                         start=(dc == 0), stop=(dc == 7))

        sch.emit("tensor", statmm2)
        for qh in range(8):
            qb2 = qh % 2
            sch.wait("tensor", "act", t_sq[qh])

            def sqmm2(t, qh=qh, qb2=qb2):
                t.matmul(psSq2[:, :], ones_col_bf[:, :],
                         x2q[qb2][:, :],
                         start=(qh == 0), stop=(qh == 7)).then_inc(SEMS["pe"], 1)

            sch.emit("tensor", sqmm2)
            x2q_pe[qb2] = sch.inc("pe")
            assert x2q_pe[qb2] == pe_base + qh + 1
        t_stat = x2q_pe[1]
        xp_pe[pb] = t_stat

        sch.wait("vector", "pe", t_stat)
        sch.wait("vector", "pe", rows2_pe)
        sch.wait("vector", "act", rows2_act)

        def rows1b(v):
            v.tensor_scalar(srow2[:, 0:512], psSum2[:, :], -1.0 / D, None,
                            ALU.mult).then_inc(SEMS["dve"], 1)
            v.tensor_scalar(srow2[:, 512:1024], psSq2[:, :], 1.0 / D, None,
                            ALU.mult).then_inc(SEMS["dve"], 1)

        sch.emit("vector", rows1b)
        t_r1s = sch.inc("dve", 2)
        rows2_dve = t_r1s
        sch.wait("vector", "dve", t_r1s)
        sch.emit("vector", lambda v: v.tensor_tensor(
            srow2[:, 1024:1536], srow2[:, 0:512], srow2[:, 0:512], ALU.mult)
            .then_inc(SEMS["dve"], 1))
        t_m2r = sch.inc("dve")
        sch.wait("vector", "dve", t_m2r)
        sch.emit("vector", lambda v: v.tensor_tensor(
            srow2[:, 1024:1536], srow2[:, 512:1024], srow2[:, 1024:1536],
            ALU.subtract).then_inc(SEMS["dve"], 1))
        t_var = sch.inc("dve")
        sch.wait("scalar", "dve", t_var)
        sch.emit("scalar", lambda s: s.activation(
            srow2[:, 512:1024], srow2[:, 1024:1536], AF.Ln,
            bias=eps_col[0:1, :]).then_inc(SEMS["act"], 1))
        t_ln = sch.inc("act")
        sch.wait("scalar", "act", t_ln)
        sch.emit("scalar", lambda s: s.activation(
            srow2[:, 1024:1536], srow2[:, 512:1024], AF.Exp, scale=-0.5)
            .then_inc(SEMS["act"], 1))
        t_rstd = sch.inc("act")
        rows2_act = t_rstd
        sch.wait("vector", "act", t_rstd)
        sch.emit("vector", lambda v: v.tensor_tensor(
            srow2[:, 0:512], srow2[:, 0:512], srow2[:, 1024:1536], ALU.mult)
            .then_inc(SEMS["dve"], 1))
        t_nmr = sch.inc("dve")
        rows2_dve = t_nmr

        sch.wait("tensor", "act", t_rstd)
        sch.wait("tensor", "dve", t_nmr)
        sch.wait("tensor", "dve", xnp_dve[1 - pb])

        def bcast2(t):
            t.matmul(psB12[:, :], ones_row_f[:, :], srow2[:, 1024:1536],
                     start=True, stop=True)
            t.matmul(psB22[:, :], ones_row_f[:, :], srow2[:, 0:512],
                     start=True, stop=True).then_inc(SEMS["pe"], 1)

        sch.emit("tensor", bcast2)
        t_bc = sch.inc("pe")
        rows2_pe = t_bc

        sch.wait("vector", "pe", t_bc)
        sch.wait("vector", "pe", xnp_pe[pb])
        prev_b = 0
        for dc in range(8):
            sch.wait("vector", "dve", prev_b)
            sch.emit("vector", lambda v, pb=pb, dc=dc: v.tensor_tensor(
                xtp[:, :], xpch[pb][:, dc * 512:(dc + 1) * 512], psB12[:, :],
                ALU.mult).then_inc(SEMS["dve"], 1))
            t_a = sch.inc("dve")
            sch.wait("vector", "dve", t_a)
            sch.emit("vector", lambda v, pb=pb, dc=dc: v.tensor_tensor(
                xnp[pb][:, dc * 512:(dc + 1) * 512], xtp[:, :], psB22[:, :],
                ALU.add).then_inc(SEMS["dve"], 1))
            prev_b = sch.inc("dve")
        t_xn = sch.now("dve")
        xnp_dve[pb] = t_xn
        xp_dve[pb] = t_xn

        if pend_kv is not None:
            emit_kv(*pend_kv)
        pend_kv = (pb, ci, t_xn)

    if pend_kv is not None:
        emit_kv(*pend_kv)

    qev_act = list(kev_act)
    NTC = T // 512
    for hp in (range(8) if not p2_no_q else []):
        for tc in range(NTC):
            qb = (hp * NTC + tc) % 2
            sch.wait("tensor", "act", qev_act[qb])

            def qmm(t, hp=hp, tc=tc, qb=qb):
                for dc in range(8):
                    ins = t.matmul(
                        psK[qb][:, :],
                        wq_sb[:, dc * D + hp * 128: dc * D + hp * 128 + 128],
                        xn_own[:, dc * T + tc * 512: dc * T + tc * 512 + 512],
                        start=(dc == 0), stop=(dc == 7))
                ins.then_inc(SEMS["pe"], 1)

            sch.emit("tensor", qmm)
            t_qmm = sch.inc("pe")
            sch.wait("scalar", "pe", t_qmm)
            sch.emit("scalar", lambda s, hp=hp, tc=tc, qb=qb: s.copy(
                q_sb[:, hp * T + tc * 512: hp * T + tc * 512 + 512],
                psK[qb][:, :]).then_inc(SEMS["act"], 1))
            qev_act[qb] = sch.inc("act")

    if debug:
        def dbgp2(g):
            g.dma_start(out=dbg_xnp[:, 0:8 * 512], in_=xnp[0][:, :])\
                .then_inc(SEMS["ldw"], 16)
            g.dma_start(out=dbg_xnp[:, 8 * 512:], in_=xnp[1][:, :])\
                .then_inc(SEMS["ldw"], 16)
            g.dma_start(out=dbg_srow[:, :], in_=srow2[:, :])\
                .then_inc(SEMS["ldw"], 16)

        sch.wait("gpsimd", "pe", sch.now("pe"))
        sch.wait("gpsimd", "dve", sch.now("dve"))
        sch.wait("gpsimd", "act", sch.now("act"))
        sch.emit("gpsimd", dbgp2)
        sch.inc("ldw", 48)

    es_p2.close()
    ps_p2.close()
    sch.barrier()

    run_p3 = stop_after >= 3
    es_p3, ps_p3 = ExitStack(), ExitStack()
    tbl_sb = sb(es_p3, "tbl_sb", [1, NSLOT_TOT // 4], I32)
    kring = [sb(es_p3, f"kring{i}", [128, PFX], BF16) for i in range(2)]
    vring = [sb(es_p3, f"vring{i}", [128, NKT * 130], BF16) for i in range(2)]
    kst = [sb(es_p3, f"kst{i}", [128, 128], BF16) for i in range(3)]
    qst = [sb(es_p3, f"qst{i}", [128, 512], BF16) for i in range(3)]
    vst = [sb(es_p3, f"vst{i}", [128, 256], BF16) for i in range(4)]
    pt2 = [sb(es_p3, f"pt2_{i}", [128, 1024], BF16) for i in range(4)]
    trid = sb(es_p3, "trid", [128, 3 * 512], BF16)
    pv_sb = sb(es_p3, "pv_sb", [65, 1024], BF16)
    rl_sb = sb(es_p3, "rl_sb", [128, 1], F32)
    psS2 = [ps(ps_p3, f"psS2_{i}", [128, 1024]) for i in range(2)]
    psCV = [ps(ps_p3, f"psCV{h}", [128, 1024]) for h in range(2)]

    P3R = {}

    def p3regs_v(v):
        for nm in ("v_rv", "v_rk", "v_rq", "v_sv"):
            P3R[nm] = v.alloc_register(nm)

    sch.emit("vector", p3regs_v)

    def p3regs_t(t):
        for nm in ("t_rv", "t_pc"):
            P3R[nm] = t.alloc_register(nm)

    sch.emit("tensor", p3regs_t)

    ring_ld = [0, 0]
    ring_pe_use = [0, 0]
    ring_dve_use = [0, 0]

    def emit_ring_load(r):
        rb = r % 2
        hp = r // NQB
        sch.wait("gpsimd", "pe", ring_pe_use[rb])
        sch.wait("gpsimd", "dve", ring_dve_use[rb])

        def rload(g, rb=rb, hp=hp):
            g.dma_start(out=kring[rb][:, :],
                        in_=bass.AP(k_spill, hp * 128 * PFX,
                                    [[PFX, 128], [1, PFX]])
                        ).then_inc(SEMS["ldr%d" % rb], 16)
            g.dma_start(out=vring[rb].ap().rearrange(
                "p (n w) -> p n w", w=130),
                in_=bass.AP(v_spill,
                            (hp // 2) * 128 * NKT * 260 + (hp % 2) * 130,
                            [[NKT * 260, 128], [260, NKT], [1, 130]])
                ).then_inc(SEMS["ldr%d" % rb], 16)

        sch.emit("gpsimd", rload)
        ring_ld[rb] = sch.inc("ldr%d" % rb, 32)

    if run_p3:
        sch.emit("gpsimd", lambda g: g.dma_start(
            out=tbl_sb[:, :], in_=tbl[:, :]).then_inc(SEMS["ldw"], 16))
        t_tbl = sch.inc("ldw", 16)
        sch.wait("vector", "ldw", t_tbl)
        sch.wait("tensor", "ldw", t_tbl)

        dve_base0 = sch.now("dve")

        def vst_clear(v, dve_base0=dve_base0):
            for i3 in range(4):
                v.memset(vst[i3][:, :], 0.0).then_inc(SEMS["dve"], 1)
            v.memset(trid[:, :], 1.0).then_inc(SEMS["dve"], 1)
            v.wait_ge(SEMS["dve"], dve_base0 + 5)
            for d_ in range(1, 4):
                v.memset(trid[:, (d_ - 1) * 512: (d_ - 1) * 512 + 128 * d_],
                         0.0).then_inc(SEMS["dve"], 1)
                v.tensor_copy(trid[:, (d_ - 1) * 512 + 128 * d_:
                                   (d_ - 1) * 512 + 128 * d_ + 128],
                              tri[:, :]).then_inc(SEMS["dve"], 1)

        sch.emit("vector", vst_clear)
        t_vclear = sch.inc("dve", 11)
        emit_ring_load(0)
        emit_ring_load(1)
    else:
        t_vclear = 0

    tbl_pos = 0
    kq_dve = [0, 0, 0, 0]
    smm_pe3 = [0, 0, 0]
    pav_pe4 = [0, 0, 0, 0]
    psS_war = [0, 0]
    pt_dve4 = [0, 0, 0, 0]
    ctx_evac_dve = 0
    t_exp_last = 0

    for r in range(NR if run_p3 else 0):
        hp, i = r // NQB, r % NQB
        rb = r % 2
        nslot = NSLOT[i]
        sch.wait("vector", "ldr%d" % rb, ring_ld[rb])
        sch.wait("tensor", "ldr%d" % rb, ring_ld[rb])
        qbase = hp * T + 512 * i
        pav_last = 0

        pend = []

        def emit_pav(p, skip_act=False):
            slot_idx_p, b4_p, st_p, t_exp_p, t_msk_p = p
            if not skip_act:
                sch.wait("tensor", "act", t_exp_p)
            if t_msk_p:
                sch.wait("tensor", "dve", t_msk_p)
            if st_p:
                sch.wait("tensor", "dve", ctx_evac_dve)

            def pavf(t, slot_idx=slot_idx_p, b4=b4_p, st_flag=st_p):
                t.reg_load(P3R["t_rv"],
                           tbl_sb[0:1, slot_idx // 4:slot_idx // 4 + 1])
                if slot_idx % 4:
                    t.reg_alu(P3R["t_rv"], P3R["t_rv"], 8 * (slot_idx % 4),
                              ALU.arith_shift_right)
                t.reg_alu(P3R["t_pc"], P3R["t_rv"], 1, ALU.bitwise_and)
                t.reg_alu(P3R["t_pc"], P3R["t_pc"], 9, ALU.logical_shift_left)
                ins = None
                for h in range(2):
                    ins = t.matmul(
                        bass.AP(psCV[h], P3R["t_pc"], [[1024, 128], [1, 512]]),
                        vst[b4][:, h * 65:h * 65 + 128],
                        pt2[b4][:, h * 512:(h + 1) * 512],
                        start=st_flag, stop=False, skip_group_check=True)
                ins.then_inc(SEMS["pe"], 1)

            sch.emit("tensor", pavf)
            t_pav = sch.inc("pe")
            pav_pe4[b4_p] = t_pav
            return t_pav

        for s in range(nslot):
            b3 = s % 3
            b4 = s % 4
            sbank = s % 2
            is_diag = s < 8
            d = (s if s < 4 else s - 4) if is_diag else 0
            slot_idx = tbl_pos + s

            sch.wait("vector", "pe", max(smm_pe3[b3], pav_pe4[b4]))
            if r == 0 and s < 4:
                sch.wait("vector", "dve", t_vclear)

            def stage(v, slot_idx=slot_idx, b3=b3, b4=b4, qbase=qbase, rb=rb):
                v.reg_load(P3R["v_rv"],
                           tbl_sb[0:1, slot_idx // 4:slot_idx // 4 + 1])
                if slot_idx % 4:
                    v.reg_alu(P3R["v_rv"], P3R["v_rv"], 8 * (slot_idx % 4),
                              ALU.arith_shift_right)
                v.reg_alu(P3R["v_rv"], P3R["v_rv"], 255, ALU.bitwise_and)
                v.reg_alu(P3R["v_rk"], P3R["v_rv"], 1, ALU.arith_shift_right)
                v.reg_alu(P3R["v_sv"], P3R["v_rk"], 130, ALU.mult)
                v.reg_alu(P3R["v_rk"], P3R["v_rk"], 7, ALU.logical_shift_left)
                v.reg_alu(P3R["v_rq"], P3R["v_rv"], 1, ALU.bitwise_and)
                v.reg_alu(P3R["v_rq"], P3R["v_rq"], CHSH, ALU.logical_shift_left)
                v.reg_alu(P3R["v_rq"], P3R["v_rq"], qbase, ALU.add)
                v.tensor_copy(kst[b3][:, :],
                              bass.AP(kring[rb], P3R["v_rk"],
                                      [[PFX, 128], [1, 128]]))\
                    .then_inc(SEMS["dve"], 1)
                v.tensor_copy(qst[b3][:, :],
                              bass.AP(q_sb, P3R["v_rq"],
                                      [[8 * T, 128], [1, 512]]))\
                    .then_inc(SEMS["dve"], 1)
                v.tensor_copy(vst[b4][:, 0:130],
                              bass.AP(vring[rb], P3R["v_sv"],
                                      [[NKT * 130, 128], [1, 130]]))\
                    .then_inc(SEMS["dve"], 1)

            sch.emit("vector", stage)
            kq_dve[b4] = sch.inc("dve", 3)

            sch.wait("tensor", "dve", kq_dve[b4])
            sch.wait("tensor", "act", psS_war[sbank])
            if s < 2:
                sch.wait("tensor", "dve", ctx_evac_dve)

            def smm(t, b3=b3, sbank=sbank):
                t.matmul(psS2[sbank][:, 0:512], kst[b3][0:64, :],
                         qst[b3][0:64, :], start=True, stop=True,
                         tile_position=(0, 0))
                t.matmul(psS2[sbank][:, 512:1024], kst[b3][64:128, :],
                         qst[b3][64:128, :], start=True, stop=True,
                         tile_position=(64, 0)).then_inc(SEMS["pe"], 1)

            sch.emit("tensor", smm)
            t_smm = sch.inc("pe")
            smm_pe3[b3] = t_smm

            if len(pend) == 2:
                pav_last = emit_pav(pend.pop(0), skip_act=True)

            sch.wait("scalar", "pe", t_smm)
            sch.wait("scalar", "dve", pt_dve4[b4])
            sch.emit("scalar", lambda s2, sbank=sbank, b4=b4:
                     s2.activation(pt2[b4][:, :], psS2[sbank][:, :], AF.Exp,
                                   scale=0.125, bias=nshift_col[:, :])
                     .then_inc(SEMS["act"], 1))
            t_exp = sch.inc("act")
            psS_war[sbank] = t_exp
            t_exp_last = t_exp

            t_msk = 0
            if is_diag:
                sch.wait("vector", "act", t_exp)
                for h in range(2):
                    if d == 0:
                        sch.emit("vector", lambda v, h=h, b4=b4:
                                 v.tensor_tensor(
                                     pt2[b4][:, h * 512: h * 512 + 128],
                                     pt2[b4][:, h * 512: h * 512 + 128],
                                     tri[:, :], ALU.mult)
                                 .then_inc(SEMS["dve"], 1))
                    else:
                        sch.emit("vector", lambda v, h=h, b4=b4, d=d:
                                 v.tensor_tensor(
                                     pt2[b4][:, h * 512:(h + 1) * 512],
                                     pt2[b4][:, h * 512:(h + 1) * 512],
                                     trid[:, (d - 1) * 512: d * 512],
                                     ALU.mult).then_inc(SEMS["dve"], 1))
                    sch.inc("dve")
                t_msk = sch.now("dve")
                pt_dve4[b4] = t_msk

            pend.append((slot_idx, b4, s == 0 or s == 4, t_exp, t_msk))

        for p in pend:
            pav_last = emit_pav(p)
        pend = []

        tbl_pos += nslot
        ring_dve_use[rb] = max(kq_dve)

        tr_pe = 0
        for h in range(2):
            sch.wait("vector", "pe", pav_last if h == 0 else tr_pe)
            sch.emit("vector", lambda v, h=h: v.tensor_copy(
                pv_sb[:, :], psCV[h][0:65, :]).then_inc(SEMS["dve"], 1))
            t_pv = sch.inc("dve")

            sch.wait("tensor", "dve", t_pv)
            if h == 0:
                sch.wait("tensor", "act", t_exp_last)

            def trf(t, h=h):
                for chunk in range(2):
                    for j in range(4):
                        ins = t.matmul(
                            psS2[h][:, chunk * 512 + j * 65:
                                    chunk * 512 + j * 65 + 65],
                            pv_sb[:, chunk * 512 + j * 128:
                                  chunk * 512 + j * 128 + 128],
                            ident[0:65, 0:65], start=(j == 0), stop=(j == 3),
                            skip_group_check=True)
                ins.then_inc(SEMS["pe"], 1)

            sch.emit("tensor", trf)
            tr_pe = sch.inc("pe")

            sch.wait("vector", "pe", tr_pe)
            prev_ev = 0
            for chunk in range(2):
                for j in range(4):
                    base = chunk * 512 + j * 65
                    tile = chunk * (CH // 128) + 4 * i + j
                    col = tile * D + (2 * hp + h) * 64
                    sch.wait("vector", "dve", prev_ev)
                    sch.emit("vector", lambda v, h=h, base=base: v.reciprocal(
                        rl_sb[:, :], psS2[h][:, base + 64: base + 65])
                        .then_inc(SEMS["dve"], 1))
                    tk2 = sch.inc("dve")
                    sch.wait("vector", "dve", tk2)
                    sch.emit("vector", lambda v, h=h, base=base, col=col:
                             v.tensor_scalar(
                                 ctx_tm[:, col: col + 64],
                                 psS2[h][:, base: base + 64], rl_sb[:, :],
                                 None, ALU.mult).then_inc(SEMS["dve"], 1))
                    prev_ev = sch.inc("dve")
        ctx_evac_dve = sch.now("dve")

        if r + 2 < NR:
            emit_ring_load(r + 2)

    es_p3.close()
    ps_p3.close()
    es_attn.close()
    sch.barrier()

    es_tail = ExitStack()
    out_acc = sb(es_tail, "out_acc", [128, NQT * D], F32)
    coef_sb = sb(es_tail, "coef_sb", [128, NQT * NE], F32)

    es_p4, ps_p4 = ExitStack(), ExitStack()
    wg_buf = [sb(es_p4, f"wg{i}", [128, 8 * IE], BF16) for i in range(2)]
    wu_buf = [sb(es_p4, f"wu{i}", [128, 8 * IE], BF16) for i in range(2)]
    wd_buf = [sb(es_p4, f"wd{i}", [128, 4 * D], BF16) for i in range(2)]
    sg_buf = [sb(es_p4, f"sg{i}", [128, 512], BF16) for i in range(2)]
    h_sb = sb(es_p4, "h_sb", [128, 4 * 512], BF16)
    psGx = [ps(ps_p4, f"psGx{i}", [128, 512]) for i in range(2)]
    psUx = [ps(ps_p4, f"psUx{i}", [128, 512]) for i in range(2)]
    psD = [ps(ps_p4, f"psD{i}", [128, 512]) for i in range(2)]
    psWM = ps(ps_p4, "psWM", [128, NE])

    for tt in range(NQT):
        sch.emit("gpsimd", lambda g, tt=tt: g.dma_start(
            out=out_acc[:, tt * D:(tt + 1) * D],
            in_=x_own[tt * 128:(tt + 1) * 128, :]).then_inc(SEMS["ldxo"], 16))
    xo_ld = sch.inc("ldxo", 16 * NQT)

    sch.wait("tensor", "dve", t_wmean)
    sch.emit("tensor", lambda t: t.matmul(
        psWM[:, :], ones_row_f[:, :], wmean_row[:, :], start=True, stop=True)
        .then_inc(SEMS["pe"], 1))
    t_wmb = sch.inc("pe")
    sch.wait("vector", "pe", t_wmb)
    for tt in range(NQT):
        sch.emit("vector", lambda v, tt=tt: v.tensor_tensor(
            coef_sb[:, tt * NE:(tt + 1) * NE],
            mm_sb[:, tt * NE:(tt + 1) * NE], psWM[:, :], ALU.mult)
            .then_inc(SEMS["dve"], 1))
        sch.inc("dve")
    t_coef = sch.now("dve")

    def emit_wload(eidx, wb):
        def wload(g, eidx=eidx, wb=wb):
            if eidx < 8:
                gsrc = bass.AP(egwT, eidx * D * IE,
                               [[IE, 128], [128 * IE, 8], [1, IE]])
                usrc = bass.AP(euwT, eidx * D * IE,
                               [[IE, 128], [128 * IE, 8], [1, IE]])
                dsrc = bass.AP(edwT, eidx * IE * D,
                               [[D, 128], [128 * D, 4], [1, D]])
            else:
                k = eidx - 8
                gsrc = bass.AP(sgwT, 512 * k,
                               [[2 * IE, 128], [128 * 2 * IE, 8], [1, IE]])
                usrc = bass.AP(suwT, 512 * k,
                               [[2 * IE, 128], [128 * 2 * IE, 8], [1, IE]])
                dsrc = bass.AP(sdwT, 512 * k * D,
                               [[D, 128], [128 * D, 4], [1, D]])
            g.dma_start(out=wg_buf[wb].ap().rearrange("p (c w) -> p c w", c=8),
                        in_=gsrc).then_inc(SEMS["ldwe%d" % wb], 16)
            g.dma_start(out=wu_buf[wb].ap().rearrange("p (c w) -> p c w", c=8),
                        in_=usrc).then_inc(SEMS["ldwe%d" % wb], 16)
            g.dma_start(out=wd_buf[wb].ap().rearrange("p (c w) -> p c w", c=4),
                        in_=dsrc).then_inc(SEMS["ldwe%d" % wb], 16)

        sch.emit("gpsimd", wload)
        return sch.inc("ldwe%d" % wb, 48)

    wld = [0, 0]
    if stop_after >= 4:
        wld[0] = emit_wload(0, 0)

    acc_tok = {}
    wbuf_pe = [0, 0]
    psg_war4 = [0, 0]
    psu_war4 = [0, 0]
    psd_war4 = [0, 0]
    h_pe = 0
    sg_dve4 = [0, 0]

    for eidx in range(10 if stop_after >= 4 else 0):
        wb = eidx % 2
        if eidx < 9:
            sch.wait("gpsimd", "pe", wbuf_pe[1 - wb])
            wld[1 - wb] = emit_wload(eidx + 1, 1 - wb)
        sch.wait("tensor", "ldwe%d" % wb, wld[wb])

        for tb in range(T // 512):
            t_h_last = 0
            for ic in range(4):
                gb = ic % 2
                sch.wait("tensor", "act", psg_war4[gb])
                sch.wait("tensor", "dve", psu_war4[gb])
                sch.wait("tensor", "pe", h_pe)

                def gumm(t, tb=tb, ic=ic, gb=gb, wb=wb):
                    for dc in range(8):
                        t.matmul(psGx[gb][:, :],
                                 wg_buf[wb][:, dc * IE + ic * 128:
                                            dc * IE + ic * 128 + 128],
                                 xn_own[:, dc * T + tb * 512:
                                        dc * T + tb * 512 + 512],
                                 start=(dc == 0), stop=(dc == 7))
                    for dc in range(8):
                        ins = t.matmul(
                            psUx[gb][:, :],
                            wu_buf[wb][:, dc * IE + ic * 128:
                                       dc * IE + ic * 128 + 128],
                            xn_own[:, dc * T + tb * 512:
                                   dc * T + tb * 512 + 512],
                            start=(dc == 0), stop=(dc == 7))
                    ins.then_inc(SEMS["pe"], 1)

                sch.emit("tensor", gumm)
                t_gu = sch.inc("pe")
                wbuf_pe[wb] = max(wbuf_pe[wb], t_gu)

                sch.wait("scalar", "pe", t_gu)
                sch.wait("scalar", "dve", sg_dve4[gb])
                sch.emit("scalar", lambda s, gb=gb: s.activation(
                    sg_buf[gb][:, :], psGx[gb][:, :], AF.Silu)
                    .then_inc(SEMS["act"], 1))
                t_silu = sch.inc("act")
                psg_war4[gb] = t_silu

                sch.wait("vector", "act", t_silu)
                sch.wait("vector", "pe", h_pe)
                sch.emit("vector", lambda v, gb=gb, ic=ic: v.tensor_tensor(
                    h_sb[:, ic * 512:(ic + 1) * 512], psUx[gb][:, :],
                    sg_buf[gb][:, :], ALU.mult).then_inc(SEMS["dve"], 1))
                t_h = sch.inc("dve")
                psu_war4[gb] = t_h
                sg_dve4[gb] = t_h
                t_h_last = t_h

            down_last = 0
            for j in range(4):
                tt = tb * 4 + j
                for hc in range(2):
                    db = (j * 2 + hc) % 2
                    sch.wait("tensor", "dve", t_h_last)
                    sch.wait("tensor", "dve", psd_war4[db])

                    def dmm(t, j=j, hc=hc, db=db, wb=wb):
                        for ic in range(4):
                            ins = t.matmul(
                                psD[db][:, :],
                                h_sb[:, ic * 512 + j * 128:
                                     ic * 512 + j * 128 + 128],
                                wd_buf[wb][:, ic * D + hc * 512:
                                           ic * D + hc * 512 + 512],
                                start=(ic == 0), stop=(ic == 3))
                        ins.then_inc(SEMS["pe"], 1)

                    sch.emit("tensor", dmm)
                    t_d = sch.inc("pe")
                    down_last = t_d
                    wbuf_pe[wb] = max(wbuf_pe[wb], t_d)

                    sch.wait("vector", "pe", t_d)
                    if eidx == 0:
                        sch.wait("vector", "ldxo", xo_ld)
                        sch.wait("vector", "dve", t_coef)
                    sch.wait("vector", "dve", acc_tok.get((tt, hc), 0))

                    if eidx < 8:
                        def accf(v, tt=tt, hc=hc, db=db, eidx=eidx):
                            v.scalar_tensor_tensor(
                                out_acc[:, tt * D + hc * 512:
                                        tt * D + hc * 512 + 512],
                                psD[db][:, :],
                                coef_sb[:, tt * NE + eidx: tt * NE + eidx + 1],
                                out_acc[:, tt * D + hc * 512:
                                        tt * D + hc * 512 + 512],
                                ALU.mult, ALU.add).then_inc(SEMS["dve"], 1)
                    else:
                        def accf(v, tt=tt, hc=hc, db=db):
                            v.tensor_tensor(
                                out_acc[:, tt * D + hc * 512:
                                        tt * D + hc * 512 + 512],
                                psD[db][:, :],
                                out_acc[:, tt * D + hc * 512:
                                        tt * D + hc * 512 + 512],
                                ALU.add).then_inc(SEMS["dve"], 1)

                    sch.emit("vector", accf)
                    tk3 = sch.inc("dve")
                    acc_tok[(tt, hc)] = tk3
                    psd_war4[db] = tk3
            h_pe = down_last

    es_p4.close()
    ps_p4.close()
    sch.barrier()

    es_p5, ps_p5 = ExitStack(), ExitStack()
    ctxT = sb(es_p5, "ctxT", [128, 8 * T], BF16)
    wo_sb = sb(es_p5, "wo_sb", [128, 8 * D], BF16)
    bo_sb = sb(es_p5, "bo_sb", [1, D], BF16)
    psT = [ps(ps_p5, f"psT{i}", [128, 128], BF16) for i in range(2)]
    psO = [ps(ps_p5, f"psO{i}", [128, 512]) for i in range(2)]

    def p5w(g):
        g.dma_start(out=wo_sb.ap().rearrange("p (c w) -> p c w", c=8),
                    in_=bass.AP(wo, 0, [[D, 128], [128 * D, 8], [1, D]]))\
            .then_inc(SEMS["ldwo"], 16)
        g.dma_start(out=bo_sb[:, :], in_=bo[:, :]).then_inc(SEMS["ldwo"], 16)

    sch.emit("gpsimd", p5w)
    t_wo = sch.inc("ldwo", 32)

    psT_war = [0, 0]
    for tt in range(NQT if stop_after >= 5 else 0):
        for dc in range(8):
            pb2 = dc % 2
            sch.wait("tensor", "dve", max(t_consts, psT_war[pb2]))
            sch.emit("tensor", lambda t, tt=tt, dc=dc, pb2=pb2: t.transpose(
                psT[pb2][:, :],
                ctx_tm[:, tt * D + dc * 128: tt * D + dc * 128 + 128],
                ident[:, :]).then_inc(SEMS["pe"], 1))
            t_tr = sch.inc("pe")
            sch.wait("vector", "pe", t_tr)
            sch.emit("vector", lambda v, tt=tt, dc=dc, pb2=pb2: v.tensor_copy(
                ctxT[:, dc * T + tt * 128: dc * T + tt * 128 + 128],
                psT[pb2][:, :]).then_inc(SEMS["dve"], 1))
            psT_war[pb2] = sch.inc("dve")
    t_trans_all = sch.now("dve")

    psO_war = [0, 0]
    y_st = 0
    sch.wait("tensor", "ldwo", t_wo)
    for tt in range(NQT if stop_after >= 5 else 0):
        for hc in range(2):
            ob = (tt * 2 + hc) % 2
            sch.wait("tensor", "dve", t_trans_all)
            sch.wait("tensor", "dve", psO_war[ob])

            def womm(t, tt=tt, hc=hc, ob=ob):
                for dc in range(8):
                    t.matmul(psO[ob][:, :],
                             ctxT[:, dc * T + tt * 128: dc * T + tt * 128 + 128],
                             wo_sb[:, dc * D + hc * 512:
                                   dc * D + hc * 512 + 512],
                             start=(dc == 0), stop=False)
                t.matmul(psO[ob][:, :], ones_row_bf[:, :],
                         bo_sb[:, hc * 512: hc * 512 + 512], start=False,
                         stop=True).then_inc(SEMS["pe"], 1)

            sch.emit("tensor", womm)
            t_wmm2 = sch.inc("pe")

            sch.wait("vector", "pe", t_wmm2)
            sch.wait("vector", "dve", acc_tok.get((tt, hc), 0))
            sch.emit("vector", lambda v, tt=tt, hc=hc, ob=ob: v.tensor_tensor(
                out_acc[:, tt * D + hc * 512: tt * D + hc * 512 + 512],
                psO[ob][:, :],
                out_acc[:, tt * D + hc * 512: tt * D + hc * 512 + 512],
                ALU.add).then_inc(SEMS["dve"], 1))
            tk4 = sch.inc("dve")
            acc_tok[(tt, hc)] = tk4
            psO_war[ob] = tk4

        sch.wait("sync", "dve", max(acc_tok[(tt, 0)], acc_tok[(tt, 1)]))
        sch.emit("sync", lambda sy, tt=tt: sy.dma_start(
            out=y[tt * 128:(tt + 1) * 128, :],
            in_=out_acc[:, tt * D:(tt + 1) * D]).then_inc(SEMS["st"], 16))
        y_st = sch.inc("st", 16)

    if debug:
        def dbgdump(sy):
            sy.dma_start(out=dbg_ctx[:, :], in_=ctx_tm[:, :])\
                .then_inc(SEMS["st"], 16)
            sy.dma_start(out=dbg_xn[:, :], in_=xn_own[:, :])\
                .then_inc(SEMS["st"], 16)
            sy.dma_start(out=dbg_sc[:, :], in_=sc_row[:, :])\
                .then_inc(SEMS["st"], 16)

        sch.emit("sync", dbgdump)
        y_st = sch.inc("st", 48)

    sch.wait("sync", "st", y_st)
    for s_ in SEM_NAMES:
        sch.wait("sync", s_, sch.now(s_))

    es_p5.close()
    ps_p5.close()
    es_tail.close()

    sems_ctx = ExitStack()
    for name in SEM_NAMES:
        SEMS[name] = sems_ctx.enter_context(nc.semaphore(name))

    with es_root, sems_ctx, nc.Block() as block:
        @block.tensor
        def _(t):
            for fn in sch.steps["tensor"]:
                fn(t)

        @block.vector
        def _(v):
            for fn in sch.steps["vector"]:
                fn(v)

        @block.scalar
        def _(s):
            for fn in sch.steps["scalar"]:
                fn(s)

        @block.gpsimd
        def _(g):
            for fn in sch.steps["gpsimd"]:
                fn(g)

        @block.sync
        def _(sy):
            for fn in sch.steps["sync"]:
                fn(sy)

    nc._ledger = dict(sch.cnt)
    nc.compile()
    return nc


def make_tbl(c, S):
    CH = S // 8
    words = []
    for hp in range(8):
        for i in range(CH // 512):
            for chunk, kt, d in slot_list(c, hp, i, CH):
                words.append(kt * 2 + chunk)
    w = np.asarray(words, np.uint32)
    assert w.max() < 256 and len(w) % 4 == 0
    packed = (w[0::4] | (w[1::4] << 8) | (w[2::4] << 16) | (w[3::4] << 24))
    return packed.astype(np.uint32).view(np.int32).reshape(1, -1)


def make_in_maps(inputs):
    x = np.asarray(inputs["L0_attn_x"], np.float32)
    B, S, _ = x.shape
    gw = np.asarray(inputs["L0_ffn_gate_w"], np.float32)
    egwT = np.ascontiguousarray(
        np.asarray(inputs["L0_ffn_expert_gate_w"], np.float32).transpose(0, 2, 1))
    euwT = np.ascontiguousarray(
        np.asarray(inputs["L0_ffn_expert_up_w"], np.float32).transpose(0, 2, 1))
    edwT = np.ascontiguousarray(
        np.asarray(inputs["L0_ffn_expert_down_w"], np.float32).transpose(0, 2, 1))
    sgwT = np.ascontiguousarray(
        np.asarray(inputs["L0_ffn_shared_gate_w"], np.float32).T)
    suwT = np.ascontiguousarray(
        np.asarray(inputs["L0_ffn_shared_up_w"], np.float32).T)
    sdwT = np.ascontiguousarray(
        np.asarray(inputs["L0_ffn_shared_down_w"], np.float32).T)
    wv_heads = np.asarray(inputs["L0_attn_Wv"], np.float32)
    wv_aug = np.zeros((D, NH * 65), np.float32)
    for h in range(NH):
        wv_aug[:, h * 65: h * 65 + 64] = wv_heads[:, h * 64:(h + 1) * 64]
    shared = {
        "wq": np.ascontiguousarray(np.asarray(inputs["L0_attn_Wq"], np.float32)),
        "wk": np.ascontiguousarray(np.asarray(inputs["L0_attn_Wk"], np.float32)),
        "wv": wv_aug,
        "wo": np.ascontiguousarray(np.asarray(inputs["L0_attn_Wo"], np.float32)),
        "bo": np.asarray(inputs["L0_attn_bo"], np.float32).reshape(1, D),
        "gwT": np.ascontiguousarray(gw.T),
        "lb": np.asarray(inputs["L0_ffn_lb_bias"], np.float32).reshape(1, NE),
        "egwT": egwT, "euwT": euwT, "edwT": edwT,
        "sgwT": sgwT, "suwT": suwT, "sdwT": sdwT,
    }
    maps = []
    CH = S // 8
    for c in range(8):
        own = np.concatenate([x[0, c * CH:(c + 1) * CH, :],
                              x[1, (7 - c) * CH:(8 - c) * CH, :]], axis=0)
        rest = []
        for b in range(2):
            for p in range(8):
                if (b == 0 and p == c) or (b == 1 and p == 7 - c):
                    continue
                rest.append(x[b, p * CH:(p + 1) * CH, :])
        x_perm = np.concatenate([own] + rest, axis=0)
        x_pfx = np.concatenate([x[0, : CH * (c + 1), :],
                                x[1, : CH * (8 - c), :]], axis=0)
        m = dict(shared)
        m["x_fm"] = np.ascontiguousarray(x_perm.T)
        m["x_pfx"] = np.ascontiguousarray(x_pfx.T)
        m["x_own"] = np.ascontiguousarray(own)
        m["tbl"] = make_tbl(c, S)
        maps.append(m)
    return maps


def kernel(**inputs):
    x = np.asarray(inputs["L0_attn_x"])
    B, S, _ = x.shape
    nc = build_nc(S)
    in_maps = make_in_maps(inputs)
    res = run_bass_kernel_spmd(nc, in_maps, core_ids=list(range(8)))
    CH = S // 8
    out = np.zeros((2, S, D), np.float32)
    for c in range(8):
        yv = res.results[c]["y"]
        out[0, c * CH:(c + 1) * CH, :] = yv[0:CH]
        out[1, (7 - c) * CH:(8 - c) * CH, :] = yv[CH:2 * CH]
    return out


# version 10 (final)
# speedup vs baseline: 1.0349x; 1.0349x over previous
import numpy as np
import ml_dtypes
import concourse.bass as bass
import concourse.bacc as bacc
import concourse.mybir as mybir
from concourse.bass_utils import run_bass_kernel_spmd
from contextlib import ExitStack

F32 = mybir.dt.float32
BF16 = mybir.dt.bfloat16
I32 = mybir.dt.int32
AF = mybir.ActivationFunctionType
ALU = mybir.AluOpType
AX = mybir.AxisListType

D = 1024
NH = 16
HD = 64
NE = 8
IE = 512
EXP_SHIFT = 2.0
LN_EPS = 1e-5

ENGINES = ("tensor", "vector", "scalar", "gpsimd", "sync")
SEMS = {}
SEM_NAMES = ("pe", "dve", "act", "pool", "st",
             "ldw", "ldx0", "ldx1", "ldp0", "ldp1",
             "ldk0", "ldk1", "ldv0", "ldv1", "ldr0", "ldr1",
             "ldxo", "ldwe0", "ldwe1", "ldwo",
             "lw0", "lw1", "lw2", "lw3")


class Sched:
    def __init__(self):
        self.steps = {e: [] for e in ENGINES}
        self.cnt = {s: 0 for s in SEM_NAMES}

    def inc(self, name, n=1):
        self.cnt[name] += n
        return self.cnt[name]

    def now(self, name):
        return self.cnt[name]

    def emit(self, eng, fn):
        self.steps[eng].append(fn)

    def wait(self, eng, sem, val):
        if val and val > 0:
            self.emit(eng, lambda e, s=sem, v=val: e.wait_ge(SEMS[s], v))

    def barrier(self):
        vals = [(s, self.cnt[s]) for s in SEM_NAMES]
        for eng in ENGINES:
            for s, v in vals:
                self.wait(eng, s, v)


def slot_list(c, hp, i, CH):
    ckt = CH // 128
    qoffA = ckt * c + 4 * i
    qoffB = ckt * (7 - c) + 4 * i
    bbase = ckt * (c + 1)
    slots = []
    for d in range(4):
        slots.append((0, qoffA + d, d))
    for d in range(4):
        slots.append((1, bbase + qoffB + d, d))
    for kt in range(qoffA):
        slots.append((0, kt, -1))
    for kt in range(qoffB):
        slots.append((1, bbase + kt, -1))
    assert len(slots) == 7 * ckt + 8 * i + 8
    return slots


def build_nc(S=8192, debug=False, stop_after=99, skip_p1=False,
             p2_no_k=False, p2_no_v=False, p2_no_q=False, p2_no_w=False):
    CH = S // 8
    T = 2 * CH
    PFX = 9 * CH
    NKT = PFX // 128
    NQT = T // 128
    NQB = CH // 512
    NR = 8 * NQB
    NCH1 = 2 * S // 512
    NCH2 = PFX // 512
    CHSH = CH.bit_length() - 1
    NSLOT = [7 * (CH // 128) + 8 * i + 8 for i in range(NQB)]
    NSLOT_TOT = 8 * sum(NSLOT)

    nc = bacc.Bacc("TRN2", target_bir_lowering=False,
                   dynamic_dma_scratch_size=32768)

    x_fm = nc.dram_tensor("x_fm", [D, 2 * S], F32, kind="ExternalInput")
    x_pfx = nc.dram_tensor("x_pfx", [D, PFX], F32, kind="ExternalInput")
    x_own = nc.dram_tensor("x_own", [T, D], F32, kind="ExternalInput")
    wq = nc.dram_tensor("wq", [D, D], F32, kind="ExternalInput")
    wk = nc.dram_tensor("wk", [D, D], F32, kind="ExternalInput")
    wv = nc.dram_tensor("wv", [D, NH * 65], F32, kind="ExternalInput")
    wo = nc.dram_tensor("wo", [D, D], F32, kind="ExternalInput")
    bo = nc.dram_tensor("bo", [1, D], F32, kind="ExternalInput")
    gwT = nc.dram_tensor("gwT", [D, NE], F32, kind="ExternalInput")
    lb = nc.dram_tensor("lb", [1, NE], F32, kind="ExternalInput")
    egwT = nc.dram_tensor("egwT", [NE, D, IE], F32, kind="ExternalInput")
    euwT = nc.dram_tensor("euwT", [NE, D, IE], F32, kind="ExternalInput")
    edwT = nc.dram_tensor("edwT", [NE, IE, D], F32, kind="ExternalInput")
    sgwT = nc.dram_tensor("sgwT", [D, 2 * IE], F32, kind="ExternalInput")
    suwT = nc.dram_tensor("suwT", [D, 2 * IE], F32, kind="ExternalInput")
    sdwT = nc.dram_tensor("sdwT", [2 * IE, D], F32, kind="ExternalInput")
    tbl = nc.dram_tensor("tbl", [1, NSLOT_TOT // 4], I32, kind="ExternalInput")
    y = nc.dram_tensor("y", [T, D], F32, kind="ExternalOutput")

    kind_sp = "ExternalOutput" if debug else "Internal"
    k_spill = nc.dram_tensor("k_spill", [8, 128, PFX], BF16, kind=kind_sp)
    v_spill = nc.dram_tensor("v_spill", [4, 128, NKT * 260], BF16, kind=kind_sp)
    if debug:
        dbg_ctx = nc.dram_tensor("dbg_ctx", [128, NQT * D], BF16,
                                 kind="ExternalOutput")
        dbg_xn = nc.dram_tensor("dbg_xn", [128, 8 * T], BF16,
                                kind="ExternalOutput")
        dbg_sc = nc.dram_tensor("dbg_sc", [1, 16], F32, kind="ExternalOutput")
        dbg_xnp = nc.dram_tensor("dbg_xnp", [128, 2 * 8 * 512], BF16,
                                 kind="ExternalOutput")
        dbg_srow = nc.dram_tensor("dbg_srow", [1, 3 * 512], F32,
                                  kind="ExternalOutput")

    sch = Sched()
    es_root = ExitStack()

    nc._addrmap = []

    def sb(es, name, shape, dtype):
        h = es.enter_context(nc.sbuf_tensor(name, shape, dtype))
        try:
            nbytes = shape[-1] * (2 if dtype == BF16 else 4)
            nc._addrmap.append((nc.lookup_mloc(h).addr, nbytes, name))
        except Exception:
            pass
        return h

    def ps(es, name, shape, dtype=F32):
        return es.enter_context(nc.psum_tensor(name, shape, dtype))

    tri = sb(es_root, "tri", [128, 128], BF16)
    ident = sb(es_root, "ident", [128, 128], BF16)
    id1 = sb(es_root, "id1", [1, 1], F32)
    iotaj = sb(es_root, "iotaj", [128, 128], BF16)
    iotai = sb(es_root, "iotai", [128, 1], F32)
    ones_col_bf = sb(es_root, "ones_col_bf", [128, 1], BF16)
    ones_col_f = sb(es_root, "ones_col_f", [128, 1], F32)
    ones_row_f = sb(es_root, "ones_row_f", [1, 128], F32)
    id65 = sb(es_root, "id65", [65, 65], F32)
    ones_row_bf = sb(es_root, "ones_row_bf", [1, 128], BF16)
    eps_col = sb(es_root, "eps_col", [128, 1], F32)
    nshift_col = sb(es_root, "nshift_col", [128, 1], F32)
    lb_sb = sb(es_root, "lb_sb", [1, NE], F32)
    gw_sb = sb(es_root, "gw_sb", [128, 8 * NE], BF16)
    csb = sb(es_root, "csb", [128, NE], F32)
    lbb = sb(es_root, "lbb", [128, NE], F32)
    cs_row = sb(es_root, "cs_row", [1, NE], F32)
    mm_sb = sb(es_root, "mm_sb", [128, NQT * NE], F32)
    sc_row = sb(es_root, "sc_row", [1, 16], F32)
    wmean_row = sb(es_root, "wmean_row", [1, NE], F32)
    xn_own = sb(es_root, "xn_own", [128, 8 * T], BF16)
    ctx_tm = sb(es_root, "ctx_tm", [128, NQT * D], BF16)

    es_attn = ExitStack()
    q_sb = sb(es_attn, "q_sb", [128, 8 * T], BF16)

    def p0(g):
        g.iota(iotaj[:, :], [[1, 128]], channel_multiplier=0,
               allow_small_or_imprecise_dtypes=True).then_inc(SEMS["pool"], 1)
        g.iota(iotai[:, :], [[1, 1]], channel_multiplier=1,
               allow_small_or_imprecise_dtypes=True).then_inc(SEMS["pool"], 1)
        g.memset(ones_col_bf[:, :], 1.0).then_inc(SEMS["pool"], 1)
        g.memset(ones_col_f[:, :], 1.0).then_inc(SEMS["pool"], 1)
        g.memset(ones_row_f[:, :], 1.0).then_inc(SEMS["pool"], 1)
        g.memset(ones_row_bf[:, :], 1.0).then_inc(SEMS["pool"], 1)
        g.memset(eps_col[:, :], LN_EPS).then_inc(SEMS["pool"], 1)
        g.memset(nshift_col[:, :], -EXP_SHIFT).then_inc(SEMS["pool"], 1)
        g.memset(id1[:, :], 1.0).then_inc(SEMS["pool"], 1)

    sch.emit("gpsimd", p0)
    t_pool0 = sch.inc("pool", 9)

    sch.wait("vector", "pool", t_pool0)
    sch.emit("vector", lambda v: v.tensor_scalar(
        tri[:, :], iotaj[:, :], iotai[:, :], None, ALU.is_ge)
        .then_inc(SEMS["dve"], 1))
    sch.inc("dve")
    sch.emit("vector", lambda v: v.tensor_scalar(
        ident[:, :], iotaj[:, :], iotai[:, :], None, ALU.is_equal)
        .then_inc(SEMS["dve"], 1))
    sch.inc("dve")
    sch.emit("vector", lambda v: v.tensor_scalar(
        id65[:, :], iotaj[0:65, 0:65], iotai[0:65, :], None, ALU.is_equal)
        .then_inc(SEMS["dve"], 1))
    t_consts = sch.inc("dve")

    def p0w(g):
        g.dma_start(out=lb_sb[:, :], in_=lb[:, :]).then_inc(SEMS["ldw"], 16)
        g.dma_start(out=gw_sb.ap().rearrange("p (c e) -> p c e", c=8),
                    in_=bass.AP(gwT, 0, [[NE, 128], [128 * NE, 8], [1, NE]])
                    ).then_inc(SEMS["ldw"], 16)

    sch.emit("gpsimd", p0w)
    t_smallw = sch.inc("ldw", 32)

    ps_p0 = ExitStack()
    psCs = ps(ps_p0, "psCs", [1, NE])
    psCb = ps(ps_p0, "psCb", [128, NE])
    psLb = ps(ps_p0, "psLb", [128, NE])

    sch.wait("tensor", "ldw", t_smallw)
    sch.wait("tensor", "pool", t_pool0)

    def csmm(t):
        for dc in range(8):
            ins = t.matmul(psCs[:, :], ones_col_bf[:, :],
                           gw_sb[:, dc * NE:(dc + 1) * NE],
                           start=(dc == 0), stop=(dc == 7))
        ins.then_inc(SEMS["pe"], 1)

    sch.emit("tensor", csmm)
    t_cs = sch.inc("pe")
    sch.wait("vector", "pe", t_cs)
    sch.emit("vector", lambda v: v.tensor_copy(cs_row[:, :], psCs[:, :])
             .then_inc(SEMS["dve"], 1))
    t_csr = sch.inc("dve")
    sch.wait("tensor", "dve", t_csr)

    def cbmm(t):
        t.matmul(psCb[:, :], ones_row_f[:, :], cs_row[:, :],
                 start=True, stop=True)
        t.matmul(psLb[:, :], ones_row_f[:, :], lb_sb[:, :],
                 start=True, stop=True).then_inc(SEMS["pe"], 1)

    sch.emit("tensor", cbmm)
    t_cbm = sch.inc("pe")
    sch.wait("vector", "pe", t_cbm)
    sch.emit("vector", lambda v: v.tensor_copy(csb[:, :], psCb[:, :])
             .then_inc(SEMS["dve"], 1))
    sch.inc("dve")
    sch.emit("vector", lambda v: v.tensor_copy(lbb[:, :], psLb[:, :])
             .then_inc(SEMS["dve"], 1))
    t_cblb = sch.inc("dve")
    ps_p0.close()

    es_p1, ps_p1 = ExitStack(), ExitStack()
    xch = [sb(es_p1, f"xch{i}", [128, 8 * 512], BF16) for i in range(2)]
    x2ch = sb(es_p1, "x2ch", [128, 8 * 512], BF16)
    xtmp = sb(es_p1, "xtmp", [128, 512], F32)
    srow = sb(es_p1, "srow", [1, 5 * 512], F32)
    rc_sb = sb(es_p1, "rc_sb", [128, 2], F32)
    lgt = sb(es_p1, "lgt", [128, NE], F32)
    sgt = sb(es_p1, "sgt", [128, NE], F32)
    r1 = sb(es_p1, "r1", [128, 1], F32)
    r2 = sb(es_p1, "r2", [128, 1], F32)
    m1 = sb(es_p1, "m1", [128, NE], F32)
    m2 = sb(es_p1, "m2", [128, NE], F32)
    s2t = sb(es_p1, "s2t", [128, NE], F32)
    den = sb(es_p1, "den", [128, 1], F32)
    w1c = sb(es_p1, "w1c", [128, 1], F32)
    w2c = sb(es_p1, "w2c", [128, 1], F32)
    wmt = sb(es_p1, "wmt", [128, 16], F32)
    psSum = ps(ps_p1, "psSum", [1, 512])
    psSq = ps(ps_p1, "psSq", [1, 512])
    psB1 = ps(ps_p1, "psB1", [128, 512])
    psB2 = ps(ps_p1, "psB2", [128, 512])
    psG = [ps(ps_p1, f"psG{i}", [128, NE]) for i in range(2)]
    psSC = ps(ps_p1, "psSC", [1, 16])
    psT2 = ps(ps_p1, "psT2", [128, 2])

    sch.wait("tensor", "dve", t_cblb)

    xc_pe = [0, 0]
    xc_dve = [0, 0]
    xc_act = [0, 0]
    xc_ld = [0, 0]
    x2_pe = 0
    rows_dve = 0
    rows_act = 0
    rows_pe = 0
    xn_dve_prev = 0
    chain_prev = 0
    psg_dve = [0, 0]
    rc_dve = 0
    rc_read = 0
    wmt_pe = 0

    for ci in (range(NCH1) if not skip_p1 else []):
        pb = ci % 2
        sch.wait("gpsimd", "pe", xc_pe[pb])
        sch.wait("gpsimd", "dve", xc_dve[pb])
        sch.wait("gpsimd", "act", xc_act[pb])

        def ldx(g, ci=ci, pb=pb):
            g.dma_start(out=xch[pb].ap().rearrange("p (c w) -> p c w", c=8),
                        in_=bass.AP(x_fm, ci * 512,
                                    [[2 * S, 128], [128 * 2 * S, 8], [1, 512]])
                        ).then_inc(SEMS["ldx%d" % pb], 16)

        sch.emit("gpsimd", ldx)
        xc_ld[pb] = sch.inc("ldx%d" % pb, 16)

        sch.wait("scalar", "ldx%d" % pb, xc_ld[pb])
        sch.wait("scalar", "pe", x2_pe)
        sch.emit("scalar", lambda s, pb=pb: s.activation(
            x2ch[:, :], xch[pb][:, :], AF.Square).then_inc(SEMS["act"], 1))
        t_sq = sch.inc("act")
        xc_act[pb] = t_sq

        sch.wait("tensor", "ldx%d" % pb, xc_ld[pb])
        sch.wait("tensor", "dve", rows_dve)

        def statmm(t, pb=pb):
            for dc in range(8):
                t.matmul(psSum[:, :], ones_col_bf[:, :],
                         xch[pb][:, dc * 512:(dc + 1) * 512],
                         start=(dc == 0), stop=(dc == 7))

        sch.emit("tensor", statmm)
        sch.wait("tensor", "act", t_sq)

        def sqmm(t):
            for dc in range(8):
                ins = t.matmul(psSq[:, :], ones_col_bf[:, :],
                               x2ch[:, dc * 512:(dc + 1) * 512],
                               start=(dc == 0), stop=(dc == 7))
            ins.then_inc(SEMS["pe"], 1)

        sch.emit("tensor", sqmm)
        t_stat = sch.inc("pe")
        x2_pe = t_stat
        xc_pe[pb] = t_stat

        sch.wait("vector", "pe", t_stat)
        sch.wait("vector", "pe", rows_pe)
        sch.wait("vector", "act", rows_act)

        def rows1(v):
            v.tensor_scalar(srow[:, 0:512], psSum[:, :], -1.0 / D, None,
                            ALU.mult).then_inc(SEMS["dve"], 1)
            v.tensor_scalar(srow[:, 512:1024], psSq[:, :], 1.0 / D, None,
                            ALU.mult).then_inc(SEMS["dve"], 1)

        sch.emit("vector", rows1)
        t_r1s = sch.inc("dve", 2)
        rows_dve = t_r1s
        sch.wait("vector", "dve", t_r1s)
        sch.emit("vector", lambda v: v.tensor_tensor(
            srow[:, 1024:1536], srow[:, 0:512], srow[:, 0:512], ALU.mult)
            .then_inc(SEMS["dve"], 1))
        t_m2r = sch.inc("dve")
        sch.wait("vector", "dve", t_m2r)
        sch.emit("vector", lambda v: v.tensor_tensor(
            srow[:, 1024:1536], srow[:, 512:1024], srow[:, 1024:1536],
            ALU.subtract).then_inc(SEMS["dve"], 1))
        t_var = sch.inc("dve")
        sch.wait("scalar", "dve", t_var)
        sch.emit("scalar", lambda s: s.activation(
            srow[:, 1536:2048], srow[:, 1024:1536], AF.Ln,
            bias=eps_col[0:1, :]).then_inc(SEMS["act"], 1))
        t_ln = sch.inc("act")
        sch.wait("scalar", "act", t_ln)
        sch.emit("scalar", lambda s: s.activation(
            srow[:, 2048:2560], srow[:, 1536:2048], AF.Exp, scale=-0.5)
            .then_inc(SEMS["act"], 1))
        t_rstd = sch.inc("act")
        rows_act = t_rstd
        sch.wait("vector", "act", t_rstd)
        sch.emit("vector", lambda v: v.tensor_tensor(
            srow[:, 0:512], srow[:, 0:512], srow[:, 2048:2560], ALU.mult)
            .then_inc(SEMS["dve"], 1))
        t_nmr = sch.inc("dve")
        rows_dve = t_nmr

        if ci < T // 512:
            sch.wait("tensor", "act", t_rstd)
            sch.wait("tensor", "dve", t_nmr)
            sch.wait("tensor", "dve", xn_dve_prev)

            def bcast(t):
                t.matmul(psB1[:, :], ones_row_f[:, :], srow[:, 2048:2560],
                         start=True, stop=True)
                t.matmul(psB2[:, :], ones_row_f[:, :], srow[:, 0:512],
                         start=True, stop=True).then_inc(SEMS["pe"], 1)

            sch.emit("tensor", bcast)
            t_bc = sch.inc("pe")
            rows_pe = max(rows_pe, t_bc)

            sch.wait("vector", "pe", t_bc)
            prev_b = 0
            for dc in range(8):
                sch.wait("vector", "dve", prev_b)
                sch.emit("vector", lambda v, pb=pb, dc=dc: v.tensor_tensor(
                    xtmp[:, :], xch[pb][:, dc * 512:(dc + 1) * 512],
                    psB1[:, :], ALU.mult).then_inc(SEMS["dve"], 1))
                t_a = sch.inc("dve")
                sch.wait("vector", "dve", t_a)
                sch.emit("vector", lambda v, ci=ci, dc=dc: v.tensor_tensor(
                    xn_own[:, dc * T + ci * 512: dc * T + ci * 512 + 512],
                    xtmp[:, :], psB2[:, :], ALU.add).then_inc(SEMS["dve"], 1))
                prev_b = sch.inc("dve")
            xn_dve_prev = sch.now("dve")
            xc_dve[pb] = xn_dve_prev

        for j in range(4):
            gb = j % 2
            tile = ci * 4 + j
            sch.wait("tensor", "dve", psg_dve[gb])

            def gmm(t, pb=pb, j=j, gb=gb):
                for dc in range(8):
                    ins = t.matmul(
                        psG[gb][:, :],
                        xch[pb][:, dc * 512 + j * 128: dc * 512 + j * 128 + 128],
                        gw_sb[:, dc * NE:(dc + 1) * NE],
                        start=(dc == 0), stop=(dc == 7))
                ins.then_inc(SEMS["pe"], 1)

            sch.emit("tensor", gmm)
            t_gmm = sch.inc("pe")
            xc_pe[pb] = t_gmm

            sch.wait("tensor", "act", t_rstd)
            sch.wait("tensor", "dve", t_nmr)
            sch.wait("tensor", "dve", rc_dve)

            def trr(t, j=j):
                t.transpose(psT2[:, 0:1],
                            srow[0:1, 2048 + j * 128: 2048 + j * 128 + 128],
                            id1[:, :])
                t.transpose(psT2[:, 1:2], srow[0:1, j * 128: j * 128 + 128],
                            id1[:, :]).then_inc(SEMS["pe"], 1)

            sch.emit("tensor", trr)
            t_tr = sch.inc("pe")
            rows_pe = max(rows_pe, t_tr)

            sch.wait("vector", "pe", t_tr)
            sch.wait("vector", "dve", rc_read)
            sch.emit("vector", lambda v: v.tensor_copy(rc_sb[:, :], psT2[:, :])
                     .then_inc(SEMS["dve"], 1))
            rc_dve = sch.inc("dve")

            sch.wait("vector", "pe", t_gmm)
            sch.wait("vector", "dve", rc_dve)
            sch.wait("vector", "dve", chain_prev)
            sch.emit("vector", lambda v, gb=gb: v.tensor_scalar(
                lgt[:, :], psG[gb][:, :], rc_sb[:, 0:1], None, ALU.mult)
                .then_inc(SEMS["dve"], 1))
            tk = sch.inc("dve")
            psg_dve[gb] = tk
            sch.wait("vector", "dve", tk)
            sch.emit("vector", lambda v: v.scalar_tensor_tensor(
                lgt[:, :], csb[:, :], rc_sb[:, 1:2], lgt[:, :], ALU.mult,
                ALU.add).then_inc(SEMS["dve"], 1))
            tk = sch.inc("dve")
            sch.wait("vector", "dve", tk)
            sch.emit("vector", lambda v: v.tensor_tensor(
                lgt[:, :], lgt[:, :], lbb[:, :], ALU.add)
                .then_inc(SEMS["dve"], 1))
            t_lg = sch.inc("dve")
            rc_read = t_lg

            sch.wait("scalar", "dve", t_lg)
            sch.wait("scalar", "dve", chain_prev)
            sch.emit("scalar", lambda s: s.activation(
                sgt[:, :], lgt[:, :], AF.Sigmoid).then_inc(SEMS["act"], 1))
            t_sig = sch.inc("act")

            def dve_op(fn):
                sch.emit("vector",
                         lambda v, fn=fn: fn(v).then_inc(SEMS["dve"], 1))
                return sch.inc("dve")

            sch.wait("vector", "act", t_sig)
            sch.wait("vector", "pe", wmt_pe)
            tk = dve_op(lambda v: v.reduce_max(r1[:, :], sgt[:, :], axis=AX.X))
            sch.wait("vector", "dve", tk)
            tk = dve_op(lambda v: v.tensor_scalar(m1[:, :], sgt[:, :],
                                                  r1[:, :], None, ALU.is_ge))
            sch.wait("vector", "dve", tk)
            tk = dve_op(lambda v: v.scalar_tensor_tensor(
                s2t[:, :], m1[:, :], -1e9, sgt[:, :], ALU.mult, ALU.add))
            sch.wait("vector", "dve", tk)
            tk = dve_op(lambda v: v.reduce_max(r2[:, :], s2t[:, :], axis=AX.X))
            sch.wait("vector", "dve", tk)
            tk = dve_op(lambda v: v.tensor_scalar(m2[:, :], s2t[:, :],
                                                  r2[:, :], None, ALU.is_ge))
            tk = dve_op(lambda v: v.tensor_tensor(den[:, :], r1[:, :],
                                                  r2[:, :], ALU.add))
            sch.wait("vector", "dve", tk)
            tk = dve_op(lambda v: v.tensor_scalar(den[:, :], den[:, :], 1e-8,
                                                  None, ALU.add))
            sch.wait("vector", "dve", tk)
            tk = dve_op(lambda v: v.reciprocal(den[:, :], den[:, :]))
            sch.wait("vector", "dve", tk)
            tk = dve_op(lambda v: v.tensor_tensor(w1c[:, :], r1[:, :],
                                                  den[:, :], ALU.mult))
            tk = dve_op(lambda v: v.tensor_tensor(w2c[:, :], r2[:, :],
                                                  den[:, :], ALU.mult))
            sch.wait("vector", "dve", tk)
            tk = dve_op(lambda v: v.tensor_scalar(wmt[:, 0:8], m1[:, :],
                                                  w1c[:, :], None, ALU.mult))
            sch.wait("vector", "dve", tk)
            tk = dve_op(lambda v: v.scalar_tensor_tensor(
                wmt[:, 0:8], m2[:, :], w2c[:, :], wmt[:, 0:8], ALU.mult,
                ALU.add))
            tk = dve_op(lambda v: v.tensor_tensor(wmt[:, 8:16], m1[:, :],
                                                  m2[:, :], ALU.add))
            if tile < NQT:
                sch.wait("vector", "dve", tk)
                tk = dve_op(lambda v, tile=tile: v.tensor_copy(
                    mm_sb[:, tile * NE:(tile + 1) * NE], wmt[:, 8:16]))
            chain_prev = tk

            sch.wait("tensor", "dve", chain_prev)
            sch.emit("tensor", lambda t, tile=tile: t.matmul(
                psSC[:, :], ones_col_f[:, :], wmt[:, :],
                start=(tile == 0), stop=(tile == NCH1 * 4 - 1),
                skip_group_check=True).then_inc(SEMS["pe"], 1))
            wmt_pe = sch.inc("pe")

    if not skip_p1:
        sch.wait("vector", "pe", wmt_pe)
        sch.emit("vector", lambda v: v.tensor_copy(sc_row[:, :], psSC[:, :])
                 .then_inc(SEMS["dve"], 1))
        tk = sch.inc("dve")
        sch.wait("vector", "dve", tk)
        sch.emit("vector", lambda v: v.tensor_scalar(
            sc_row[:, 8:16], sc_row[:, 8:16], 1.0, None, ALU.max)
            .then_inc(SEMS["dve"], 1))
        tk = sch.inc("dve")
        sch.wait("vector", "dve", tk)
        sch.emit("vector", lambda v: v.reciprocal(sc_row[:, 8:16],
                                                  sc_row[:, 8:16])
                 .then_inc(SEMS["dve"], 1))
        tk = sch.inc("dve")
        sch.wait("vector", "dve", tk)
        sch.emit("vector", lambda v: v.tensor_tensor(
            wmean_row[:, :], sc_row[:, 0:8], sc_row[:, 8:16], ALU.mult)
            .then_inc(SEMS["dve"], 1))
        t_wmean = sch.inc("dve")
    else:
        t_wmean = 0

    es_p1.close()
    ps_p1.close()
    sch.barrier()

    es_p2, ps_p2 = ExitStack(), ExitStack()
    wq_sb = sb(es_p2, "wq_sb", [128, 8 * D], BF16)
    wk_sb = sb(es_p2, "wk_sb", [128, 8 * D], BF16)
    wv_sb = sb(es_p2, "wv_sb", [128, 8 * NH * 65], BF16)
    xpch = [sb(es_p2, f"xpch{i}", [128, 8 * 512], BF16) for i in range(2)]
    x2q = [sb(es_p2, f"x2q{i}", [128, 512], BF16) for i in range(2)]
    xtp = sb(es_p2, "xtp", [128, 512], BF16)
    srow2 = sb(es_p2, "srow2", [1, 3 * 512], F32)
    xnp = [sb(es_p2, f"xnp{i}", [128, 8 * 512], BF16) for i in range(2)]
    kev = [sb(es_p2, f"kev{i}", [128, 512], BF16) for i in range(2)]
    vev = [sb(es_p2, f"vev{i}", [128, 260], BF16) for i in range(2)]
    psSum2 = ps(ps_p2, "psSum2", [1, 512])
    psSq2 = ps(ps_p2, "psSq2", [1, 512])
    psB12 = ps(ps_p2, "psB12", [128, 512])
    psB22 = ps(ps_p2, "psB22", [128, 512])
    psK = [ps(ps_p2, f"psK{i}", [128, 512]) for i in range(2)]
    psV = [ps(ps_p2, f"psV{i}", [128, 260]) for i in range(2)]

    def p2w(g):
        i = 0
        for dst, src, w in ((wq_sb, wq, D), (wk_sb, wk, D),
                            (wv_sb, wv, NH * 65)):
            for c in range(8):
                stream = i % 4
                if i >= 4:
                    g.wait_ge(SEMS["lw%d" % stream], 16 * (i // 4))
                g.dma_start(out=dst[:, c * w:(c + 1) * w],
                            in_=bass.AP(src, c * 128 * w, [[w, 128], [1, w]])
                            ).then_inc(SEMS["lw%d" % stream], 16)
                i += 1

    if not p2_no_w:
        sch.emit("gpsimd", p2w)
        for stream in range(4):
            sch.wait("tensor", "lw%d" % stream, sch.inc("lw%d" % stream, 96))

    xp_pe = [0, 0]
    xp_dve = [0, 0]
    xp_act = [0, 0]
    xp_ld = [0, 0]
    pend_kv = None

    def emit_kv(pb, ci, t_xn):
        for hp in (range(8) if not p2_no_k else []):
            kb = hp % 2
            sch.wait("tensor", "dve", t_xn)
            sch.wait("tensor", "act", kev_act[kb])

            def kmm(t, pb=pb, hp=hp, kb=kb):
                for dc in range(8):
                    ins = t.matmul(
                        psK[kb][:, :],
                        wk_sb[:, dc * D + hp * 128: dc * D + hp * 128 + 128],
                        xnp[pb][:, dc * 512:(dc + 1) * 512],
                        start=(dc == 0), stop=(dc == 7))
                ins.then_inc(SEMS["pe"], 1)

            sch.emit("tensor", kmm)
            t_kmm = sch.inc("pe")

            sch.wait("scalar", "pe", t_kmm)
            sch.wait("scalar", "ldk%d" % kb, kev_ld[kb])
            sch.emit("scalar", lambda s, kb=kb: s.copy(
                kev[kb][:, :], psK[kb][:, :]).then_inc(SEMS["act"], 1))
            kev_act[kb] = sch.inc("act")

            sch.wait("sync", "act", kev_act[kb])

            def kspill(g, ci=ci, hp=hp, kb=kb):
                g.dma_start(
                    out=bass.AP(k_spill, hp * 128 * PFX + ci * 512,
                                [[PFX, 128], [1, 512]]),
                    in_=kev[kb][:, :]).then_inc(SEMS["ldk%d" % kb], 16)

            sch.emit("sync", kspill)
            kev_ld[kb] = sch.inc("ldk%d" % kb, 16)

        for j in (range(4) if not p2_no_v else []):
            kt = ci * 4 + j
            for gi in range(4):
                vb = (j * 4 + gi) % 2
                sch.wait("tensor", "dve", vev_dve[vb])

                def vmm(t, pb=pb, j=j, gi=gi, vb=vb):
                    for dc in range(8):
                        ins = t.matmul(
                            psV[vb][:, :],
                            xnp[pb][:, dc * 512 + j * 128:
                                    dc * 512 + j * 128 + 128],
                            wv_sb[:, dc * (NH * 65) + gi * 260:
                                  dc * (NH * 65) + gi * 260 + 260],
                            start=(dc == 0), stop=(dc == 7))
                    ins.then_inc(SEMS["pe"], 1)

                sch.emit("tensor", vmm)
                t_vmm = sch.inc("pe")

                sch.wait("vector", "pe", t_vmm)
                sch.wait("vector", "ldv%d" % vb, vev_ld[vb])
                sch.emit("vector", lambda v, vb=vb: v.tensor_copy(
                    vev[vb][:, :], psV[vb][:, :]).then_inc(SEMS["dve"], 1))
                tk1 = sch.inc("dve")
                sch.wait("vector", "dve", tk1)
                sch.emit("vector", lambda v, vb=vb: v.memset(
                    bass.AP(vev[vb], 64, [[260, 128], [65, 4]]), 1.0)
                    .then_inc(SEMS["dve"], 1))
                vev_dve[vb] = sch.inc("dve")

                sch.wait("sync", "dve", vev_dve[vb])

                def vspill(g, kt=kt, gi=gi, vb=vb):
                    g.dma_start(
                        out=bass.AP(v_spill,
                                    gi * 128 * NKT * 260 + kt * 260,
                                    [[NKT * 260, 128], [1, 260]]),
                        in_=vev[vb][:, :]).then_inc(SEMS["ldv%d" % vb], 16)

                sch.emit("sync", vspill)
                vev_ld[vb] = sch.inc("ldv%d" % vb, 16)
        xnp_pe[pb] = sch.now("pe")
    x2q_pe = [0, 0]
    rows2_dve = 0
    rows2_act = 0
    rows2_pe = 0
    xnp_dve = [0, 0]
    xnp_pe = [0, 0]
    kev_act = [0, 0]
    kev_ld = [0, 0]
    vev_dve = [0, 0]
    vev_ld = [0, 0]

    for ci in range(NCH2):
        pb = ci % 2
        sch.wait("gpsimd", "pe", xp_pe[pb])
        sch.wait("gpsimd", "dve", xp_dve[pb])
        sch.wait("gpsimd", "act", xp_act[pb])

        def ldxp(g, ci=ci, pb=pb):
            g.dma_start(out=xpch[pb].ap().rearrange("p (c w) -> p c w", c=8),
                        in_=bass.AP(x_pfx, ci * 512,
                                    [[PFX, 128], [128 * PFX, 8], [1, 512]])
                        ).then_inc(SEMS["ldp%d" % pb], 16)

        sch.emit("gpsimd", ldxp)
        xp_ld[pb] = sch.inc("ldp%d" % pb, 16)

        t_sq = [0] * 8
        pe_base = sch.now("pe")
        for qh in range(8):
            qb2 = qh % 2
            if qh == 0:
                sch.wait("scalar", "ldp%d" % pb, xp_ld[pb])
            sch.wait("scalar", "pe",
                     x2q_pe[qb2] if qh < 2 else pe_base + qh - 1)
            sch.emit("scalar", lambda s, pb=pb, qh=qh, qb2=qb2: s.activation(
                x2q[qb2][:, :], xpch[pb][:, qh * 512:(qh + 1) * 512],
                AF.Square).then_inc(SEMS["act"], 1))
            t_sq[qh] = sch.inc("act")
        xp_act[pb] = t_sq[7]

        sch.wait("tensor", "ldp%d" % pb, xp_ld[pb])
        sch.wait("tensor", "dve", rows2_dve)

        def statmm2(t, pb=pb):
            for dc in range(8):
                t.matmul(psSum2[:, :], ones_col_bf[:, :],
                         xpch[pb][:, dc * 512:(dc + 1) * 512],
                         start=(dc == 0), stop=(dc == 7))

        sch.emit("tensor", statmm2)
        for qh in range(8):
            qb2 = qh % 2
            sch.wait("tensor", "act", t_sq[qh])

            def sqmm2(t, qh=qh, qb2=qb2):
                t.matmul(psSq2[:, :], ones_col_bf[:, :],
                         x2q[qb2][:, :],
                         start=(qh == 0), stop=(qh == 7)).then_inc(SEMS["pe"], 1)

            sch.emit("tensor", sqmm2)
            x2q_pe[qb2] = sch.inc("pe")
            assert x2q_pe[qb2] == pe_base + qh + 1
        t_stat = x2q_pe[1]
        xp_pe[pb] = t_stat

        sch.wait("vector", "pe", t_stat)
        sch.wait("vector", "pe", rows2_pe)
        sch.wait("vector", "act", rows2_act)

        def rows1b(v):
            v.tensor_scalar(srow2[:, 0:512], psSum2[:, :], -1.0 / D, None,
                            ALU.mult).then_inc(SEMS["dve"], 1)
            v.tensor_scalar(srow2[:, 512:1024], psSq2[:, :], 1.0 / D, None,
                            ALU.mult).then_inc(SEMS["dve"], 1)

        sch.emit("vector", rows1b)
        t_r1s = sch.inc("dve", 2)
        rows2_dve = t_r1s
        sch.wait("vector", "dve", t_r1s)
        sch.emit("vector", lambda v: v.tensor_tensor(
            srow2[:, 1024:1536], srow2[:, 0:512], srow2[:, 0:512], ALU.mult)
            .then_inc(SEMS["dve"], 1))
        t_m2r = sch.inc("dve")
        sch.wait("vector", "dve", t_m2r)
        sch.emit("vector", lambda v: v.tensor_tensor(
            srow2[:, 1024:1536], srow2[:, 512:1024], srow2[:, 1024:1536],
            ALU.subtract).then_inc(SEMS["dve"], 1))
        t_var = sch.inc("dve")
        sch.wait("scalar", "dve", t_var)
        sch.emit("scalar", lambda s: s.activation(
            srow2[:, 512:1024], srow2[:, 1024:1536], AF.Ln,
            bias=eps_col[0:1, :]).then_inc(SEMS["act"], 1))
        t_ln = sch.inc("act")
        sch.wait("scalar", "act", t_ln)
        sch.emit("scalar", lambda s: s.activation(
            srow2[:, 1024:1536], srow2[:, 512:1024], AF.Exp, scale=-0.5)
            .then_inc(SEMS["act"], 1))
        t_rstd = sch.inc("act")
        rows2_act = t_rstd
        sch.wait("vector", "act", t_rstd)
        sch.emit("vector", lambda v: v.tensor_tensor(
            srow2[:, 0:512], srow2[:, 0:512], srow2[:, 1024:1536], ALU.mult)
            .then_inc(SEMS["dve"], 1))
        t_nmr = sch.inc("dve")
        rows2_dve = t_nmr

        sch.wait("tensor", "act", t_rstd)
        sch.wait("tensor", "dve", t_nmr)
        sch.wait("tensor", "dve", xnp_dve[1 - pb])

        def bcast2(t):
            t.matmul(psB12[:, :], ones_row_f[:, :], srow2[:, 1024:1536],
                     start=True, stop=True)
            t.matmul(psB22[:, :], ones_row_f[:, :], srow2[:, 0:512],
                     start=True, stop=True).then_inc(SEMS["pe"], 1)

        sch.emit("tensor", bcast2)
        t_bc = sch.inc("pe")
        rows2_pe = t_bc

        sch.wait("vector", "pe", t_bc)
        sch.wait("vector", "pe", xnp_pe[pb])
        prev_b = 0
        for dc in range(8):
            sch.wait("vector", "dve", prev_b)
            sch.emit("vector", lambda v, pb=pb, dc=dc: v.tensor_tensor(
                xtp[:, :], xpch[pb][:, dc * 512:(dc + 1) * 512], psB12[:, :],
                ALU.mult).then_inc(SEMS["dve"], 1))
            t_a = sch.inc("dve")
            sch.wait("vector", "dve", t_a)
            sch.emit("vector", lambda v, pb=pb, dc=dc: v.tensor_tensor(
                xnp[pb][:, dc * 512:(dc + 1) * 512], xtp[:, :], psB22[:, :],
                ALU.add).then_inc(SEMS["dve"], 1))
            prev_b = sch.inc("dve")
        t_xn = sch.now("dve")
        xnp_dve[pb] = t_xn
        xp_dve[pb] = t_xn

        if pend_kv is not None:
            emit_kv(*pend_kv)
        pend_kv = (pb, ci, t_xn)

    if pend_kv is not None:
        emit_kv(*pend_kv)

    qev_act = list(kev_act)
    NTC = T // 512
    for hp in (range(8) if not p2_no_q else []):
        for tc in range(NTC):
            qb = (hp * NTC + tc) % 2
            sch.wait("tensor", "act", qev_act[qb])

            def qmm(t, hp=hp, tc=tc, qb=qb):
                for dc in range(8):
                    ins = t.matmul(
                        psK[qb][:, :],
                        wq_sb[:, dc * D + hp * 128: dc * D + hp * 128 + 128],
                        xn_own[:, dc * T + tc * 512: dc * T + tc * 512 + 512],
                        start=(dc == 0), stop=(dc == 7))
                ins.then_inc(SEMS["pe"], 1)

            sch.emit("tensor", qmm)
            t_qmm = sch.inc("pe")
            sch.wait("scalar", "pe", t_qmm)
            sch.emit("scalar", lambda s, hp=hp, tc=tc, qb=qb: s.copy(
                q_sb[:, hp * T + tc * 512: hp * T + tc * 512 + 512],
                psK[qb][:, :]).then_inc(SEMS["act"], 1))
            qev_act[qb] = sch.inc("act")

    if debug:
        def dbgp2(g):
            g.dma_start(out=dbg_xnp[:, 0:8 * 512], in_=xnp[0][:, :])\
                .then_inc(SEMS["ldw"], 16)
            g.dma_start(out=dbg_xnp[:, 8 * 512:], in_=xnp[1][:, :])\
                .then_inc(SEMS["ldw"], 16)
            g.dma_start(out=dbg_srow[:, :], in_=srow2[:, :])\
                .then_inc(SEMS["ldw"], 16)

        sch.wait("gpsimd", "pe", sch.now("pe"))
        sch.wait("gpsimd", "dve", sch.now("dve"))
        sch.wait("gpsimd", "act", sch.now("act"))
        sch.emit("gpsimd", dbgp2)
        sch.inc("ldw", 48)

    es_p2.close()
    ps_p2.close()
    sch.barrier()

    run_p3 = stop_after >= 3
    es_p3, ps_p3 = ExitStack(), ExitStack()
    tbl_sb = sb(es_p3, "tbl_sb", [1, NSLOT_TOT // 4], I32)
    kring = [sb(es_p3, f"kring{i}", [128, PFX], BF16) for i in range(2)]
    vring = [sb(es_p3, f"vring{i}", [128, NKT * 130], BF16) for i in range(2)]
    kst = [sb(es_p3, f"kst{i}", [128, 128], BF16) for i in range(3)]
    qst = [sb(es_p3, f"qst{i}", [128, 512], BF16) for i in range(3)]
    vst = [sb(es_p3, f"vst{i}", [128, 256], BF16) for i in range(4)]
    pt2 = [sb(es_p3, f"pt2_{i}", [128, 1024], BF16) for i in range(4)]
    trid = sb(es_p3, "trid", [128, 3 * 512], BF16)
    pv_sb = sb(es_p3, "pv_sb", [65, 1024], BF16)
    rl_sb = sb(es_p3, "rl_sb", [128, 1], F32)
    psS2 = [ps(ps_p3, f"psS2_{i}", [128, 1024]) for i in range(2)]
    psCV = [ps(ps_p3, f"psCV{h}", [128, 1024]) for h in range(2)]

    P3R = {}

    def p3regs_v(v):
        for nm in ("v_rv", "v_rk", "v_rq", "v_sv"):
            P3R[nm] = v.alloc_register(nm)

    sch.emit("vector", p3regs_v)

    def p3regs_t(t):
        for nm in ("t_rv", "t_pc"):
            P3R[nm] = t.alloc_register(nm)

    sch.emit("tensor", p3regs_t)

    ring_ld = [0, 0]
    ring_pe_use = [0, 0]
    ring_dve_use = [0, 0]

    def emit_ring_load(r):
        rb = r % 2
        hp = r // NQB
        sch.wait("sync", "pe", ring_pe_use[rb])
        sch.wait("sync", "dve", ring_dve_use[rb])

        def rload(g, rb=rb, hp=hp):
            g.dma_start(out=kring[rb][:, :],
                        in_=bass.AP(k_spill, hp * 128 * PFX,
                                    [[PFX, 128], [1, PFX]])
                        ).then_inc(SEMS["ldr%d" % rb], 16)
            g.dma_start(out=vring[rb].ap().rearrange(
                "p (n w) -> p n w", w=130),
                in_=bass.AP(v_spill,
                            (hp // 2) * 128 * NKT * 260 + (hp % 2) * 130,
                            [[NKT * 260, 128], [260, NKT], [1, 130]])
                ).then_inc(SEMS["ldr%d" % rb], 16)

        sch.emit("sync", rload)
        ring_ld[rb] = sch.inc("ldr%d" % rb, 32)

    if run_p3:
        sch.emit("gpsimd", lambda g: g.dma_start(
            out=tbl_sb[:, :], in_=tbl[:, :]).then_inc(SEMS["ldw"], 16))
        t_tbl = sch.inc("ldw", 16)
        sch.wait("vector", "ldw", t_tbl)
        sch.wait("tensor", "ldw", t_tbl)

        dve_base0 = sch.now("dve")

        def vst_clear(v, dve_base0=dve_base0):
            for i3 in range(4):
                v.memset(vst[i3][:, :], 0.0).then_inc(SEMS["dve"], 1)
            v.memset(trid[:, :], 1.0).then_inc(SEMS["dve"], 1)
            v.wait_ge(SEMS["dve"], dve_base0 + 5)
            for d_ in range(1, 4):
                v.memset(trid[:, (d_ - 1) * 512: (d_ - 1) * 512 + 128 * d_],
                         0.0).then_inc(SEMS["dve"], 1)
                v.tensor_copy(trid[:, (d_ - 1) * 512 + 128 * d_:
                                   (d_ - 1) * 512 + 128 * d_ + 128],
                              tri[:, :]).then_inc(SEMS["dve"], 1)

        sch.emit("vector", vst_clear)
        t_vclear = sch.inc("dve", 11)
        emit_ring_load(0)
        emit_ring_load(1)
    else:
        t_vclear = 0

    tbl_pos = 0
    kq_dve = [0, 0, 0, 0]
    smm_pe3 = [0, 0, 0]
    pav_pe4 = [0, 0, 0, 0]
    psS_war = [0, 0]
    pt_dve4 = [0, 0, 0, 0]
    ctx_evac_dve = 0
    t_exp_last = 0

    for r in range(NR if run_p3 else 0):
        hp, i = r // NQB, r % NQB
        rb = r % 2
        nslot = NSLOT[i]
        sch.wait("vector", "ldr%d" % rb, ring_ld[rb])
        sch.wait("tensor", "ldr%d" % rb, ring_ld[rb])
        qbase = hp * T + 512 * i
        pav_last = 0

        pend = []

        def emit_pav(p, skip_act=False):
            slot_idx_p, b4_p, st_p, t_exp_p, t_msk_p = p
            if not skip_act:
                sch.wait("tensor", "act", t_exp_p)
            if t_msk_p:
                sch.wait("tensor", "dve", t_msk_p)
            if st_p:
                sch.wait("tensor", "dve", ctx_evac_dve)

            def pavf(t, slot_idx=slot_idx_p, b4=b4_p, st_flag=st_p):
                t.reg_load(P3R["t_rv"],
                           tbl_sb[0:1, slot_idx // 4:slot_idx // 4 + 1])
                if slot_idx % 4:
                    t.reg_alu(P3R["t_rv"], P3R["t_rv"], 8 * (slot_idx % 4),
                              ALU.arith_shift_right)
                t.reg_alu(P3R["t_pc"], P3R["t_rv"], 1, ALU.bitwise_and)
                t.reg_alu(P3R["t_pc"], P3R["t_pc"], 9, ALU.logical_shift_left)
                ins = None
                for h in range(2):
                    ins = t.matmul(
                        bass.AP(psCV[h], P3R["t_pc"], [[1024, 128], [1, 512]]),
                        vst[b4][:, h * 65:h * 65 + 128],
                        pt2[b4][:, h * 512:(h + 1) * 512],
                        start=st_flag, stop=False, skip_group_check=True)
                ins.then_inc(SEMS["pe"], 1)

            sch.emit("tensor", pavf)
            t_pav = sch.inc("pe")
            pav_pe4[b4_p] = t_pav
            return t_pav

        for s in range(nslot):
            b3 = s % 3
            b4 = s % 4
            sbank = s % 2
            is_diag = s < 8
            d = (s if s < 4 else s - 4) if is_diag else 0
            slot_idx = tbl_pos + s

            sch.wait("vector", "pe", max(smm_pe3[b3], pav_pe4[b4]))
            if r == 0 and s < 4:
                sch.wait("vector", "dve", t_vclear)

            def stage(v, slot_idx=slot_idx, b3=b3, b4=b4, qbase=qbase, rb=rb):
                v.reg_load(P3R["v_rv"],
                           tbl_sb[0:1, slot_idx // 4:slot_idx // 4 + 1])
                if slot_idx % 4:
                    v.reg_alu(P3R["v_rv"], P3R["v_rv"], 8 * (slot_idx % 4),
                              ALU.arith_shift_right)
                v.reg_alu(P3R["v_rv"], P3R["v_rv"], 255, ALU.bitwise_and)
                v.reg_alu(P3R["v_rk"], P3R["v_rv"], 1, ALU.arith_shift_right)
                v.reg_alu(P3R["v_sv"], P3R["v_rk"], 130, ALU.mult)
                v.reg_alu(P3R["v_rk"], P3R["v_rk"], 7, ALU.logical_shift_left)
                v.reg_alu(P3R["v_rq"], P3R["v_rv"], 1, ALU.bitwise_and)
                v.reg_alu(P3R["v_rq"], P3R["v_rq"], CHSH, ALU.logical_shift_left)
                v.reg_alu(P3R["v_rq"], P3R["v_rq"], qbase, ALU.add)
                v.tensor_copy(kst[b3][:, :],
                              bass.AP(kring[rb], P3R["v_rk"],
                                      [[PFX, 128], [1, 128]]))\
                    .then_inc(SEMS["dve"], 1)
                v.tensor_copy(qst[b3][:, :],
                              bass.AP(q_sb, P3R["v_rq"],
                                      [[8 * T, 128], [1, 512]]))\
                    .then_inc(SEMS["dve"], 1)
                v.tensor_copy(vst[b4][:, 0:130],
                              bass.AP(vring[rb], P3R["v_sv"],
                                      [[NKT * 130, 128], [1, 130]]))\
                    .then_inc(SEMS["dve"], 1)

            sch.emit("vector", stage)
            kq_dve[b4] = sch.inc("dve", 3)

            sch.wait("tensor", "dve", kq_dve[b4])
            sch.wait("tensor", "act", psS_war[sbank])
            if s < 2:
                sch.wait("tensor", "dve", ctx_evac_dve)

            def smm(t, b3=b3, sbank=sbank):
                t.matmul(psS2[sbank][:, 0:512], kst[b3][0:64, :],
                         qst[b3][0:64, :], start=True, stop=True,
                         tile_position=(0, 0))
                t.matmul(psS2[sbank][:, 512:1024], kst[b3][64:128, :],
                         qst[b3][64:128, :], start=True, stop=True,
                         tile_position=(64, 0)).then_inc(SEMS["pe"], 1)

            sch.emit("tensor", smm)
            t_smm = sch.inc("pe")
            smm_pe3[b3] = t_smm

            if len(pend) == 2:
                pav_last = emit_pav(pend.pop(0), skip_act=True)

            sch.wait("scalar", "pe", t_smm)
            sch.wait("scalar", "dve", pt_dve4[b4])
            sch.emit("scalar", lambda s2, sbank=sbank, b4=b4:
                     s2.activation(pt2[b4][:, :], psS2[sbank][:, :], AF.Exp,
                                   scale=0.125, bias=nshift_col[:, :])
                     .then_inc(SEMS["act"], 1))
            t_exp = sch.inc("act")
            psS_war[sbank] = t_exp
            t_exp_last = t_exp

            t_msk = 0
            if is_diag:
                sch.wait("vector", "act", t_exp)
                for h in range(2):
                    if d == 0:
                        sch.emit("vector", lambda v, h=h, b4=b4:
                                 v.tensor_tensor(
                                     pt2[b4][:, h * 512: h * 512 + 128],
                                     pt2[b4][:, h * 512: h * 512 + 128],
                                     tri[:, :], ALU.mult)
                                 .then_inc(SEMS["dve"], 1))
                    else:
                        sch.emit("vector", lambda v, h=h, b4=b4, d=d:
                                 v.tensor_tensor(
                                     pt2[b4][:, h * 512:(h + 1) * 512],
                                     pt2[b4][:, h * 512:(h + 1) * 512],
                                     trid[:, (d - 1) * 512: d * 512],
                                     ALU.mult).then_inc(SEMS["dve"], 1))
                    sch.inc("dve")
                t_msk = sch.now("dve")
                pt_dve4[b4] = t_msk

            pend.append((slot_idx, b4, s == 0 or s == 4, t_exp, t_msk))

        for p in pend:
            pav_last = emit_pav(p)
        pend = []

        tbl_pos += nslot
        ring_dve_use[rb] = max(kq_dve)

        tr_pe = 0
        for h in range(2):
            sch.wait("vector", "pe", pav_last if h == 0 else tr_pe)
            sch.emit("vector", lambda v, h=h: v.tensor_copy(
                pv_sb[:, :], psCV[h][0:65, :]).then_inc(SEMS["dve"], 1))
            t_pv = sch.inc("dve")

            sch.wait("tensor", "dve", t_pv)
            if h == 0:
                sch.wait("tensor", "act", t_exp_last)

            def trf(t, h=h):
                for chunk in range(2):
                    for j in range(4):
                        ins = t.matmul(
                            psS2[h][:, chunk * 512 + j * 65:
                                    chunk * 512 + j * 65 + 65],
                            pv_sb[:, chunk * 512 + j * 128:
                                  chunk * 512 + j * 128 + 128],
                            ident[0:65, 0:65], start=(j == 0), stop=(j == 3),
                            skip_group_check=True)
                ins.then_inc(SEMS["pe"], 1)

            sch.emit("tensor", trf)
            tr_pe = sch.inc("pe")

            sch.wait("vector", "pe", tr_pe)
            prev_ev = 0
            for chunk in range(2):
                for j in range(4):
                    base = chunk * 512 + j * 65
                    tile = chunk * (CH // 128) + 4 * i + j
                    col = tile * D + (2 * hp + h) * 64
                    sch.wait("vector", "dve", prev_ev)
                    sch.emit("vector", lambda v, h=h, base=base: v.reciprocal(
                        rl_sb[:, :], psS2[h][:, base + 64: base + 65])
                        .then_inc(SEMS["dve"], 1))
                    tk2 = sch.inc("dve")
                    sch.wait("vector", "dve", tk2)
                    sch.emit("vector", lambda v, h=h, base=base, col=col:
                             v.tensor_scalar(
                                 ctx_tm[:, col: col + 64],
                                 psS2[h][:, base: base + 64], rl_sb[:, :],
                                 None, ALU.mult).then_inc(SEMS["dve"], 1))
                    prev_ev = sch.inc("dve")
        ctx_evac_dve = sch.now("dve")

        if r + 2 < NR:
            emit_ring_load(r + 2)

    es_p3.close()
    ps_p3.close()
    es_attn.close()
    sch.barrier()

    es_tail = ExitStack()
    out_acc = sb(es_tail, "out_acc", [128, NQT * D], F32)
    coef_sb = sb(es_tail, "coef_sb", [128, NQT * NE], F32)

    es_p4, ps_p4 = ExitStack(), ExitStack()
    wg_buf = [sb(es_p4, f"wg{i}", [128, 8 * IE], BF16) for i in range(2)]
    wu_buf = [sb(es_p4, f"wu{i}", [128, 8 * IE], BF16) for i in range(2)]
    wd_buf = [sb(es_p4, f"wd{i}", [128, 4 * D], BF16) for i in range(2)]
    sg_buf = [sb(es_p4, f"sg{i}", [128, 512], BF16) for i in range(2)]
    h_sb = sb(es_p4, "h_sb", [128, 4 * 512], BF16)
    psGx = [ps(ps_p4, f"psGx{i}", [128, 512]) for i in range(2)]
    psUx = [ps(ps_p4, f"psUx{i}", [128, 512]) for i in range(2)]
    psD = [ps(ps_p4, f"psD{i}", [128, 512]) for i in range(2)]
    psWM = ps(ps_p4, "psWM", [128, NE])

    for tt in range(NQT):
        sch.emit("gpsimd", lambda g, tt=tt: g.dma_start(
            out=out_acc[:, tt * D:(tt + 1) * D],
            in_=x_own[tt * 128:(tt + 1) * 128, :]).then_inc(SEMS["ldxo"], 16))
    xo_ld = sch.inc("ldxo", 16 * NQT)

    sch.wait("tensor", "dve", t_wmean)
    sch.emit("tensor", lambda t: t.matmul(
        psWM[:, :], ones_row_f[:, :], wmean_row[:, :], start=True, stop=True)
        .then_inc(SEMS["pe"], 1))
    t_wmb = sch.inc("pe")
    sch.wait("vector", "pe", t_wmb)
    for tt in range(NQT):
        sch.emit("vector", lambda v, tt=tt: v.tensor_tensor(
            coef_sb[:, tt * NE:(tt + 1) * NE],
            mm_sb[:, tt * NE:(tt + 1) * NE], psWM[:, :], ALU.mult)
            .then_inc(SEMS["dve"], 1))
        sch.inc("dve")
    t_coef = sch.now("dve")

    def emit_wload(eidx, wb):
        def wload(g, eidx=eidx, wb=wb):
            if eidx < 8:
                gsrc = bass.AP(egwT, eidx * D * IE,
                               [[IE, 128], [128 * IE, 8], [1, IE]])
                usrc = bass.AP(euwT, eidx * D * IE,
                               [[IE, 128], [128 * IE, 8], [1, IE]])
                dsrc = bass.AP(edwT, eidx * IE * D,
                               [[D, 128], [128 * D, 4], [1, D]])
            else:
                k = eidx - 8
                gsrc = bass.AP(sgwT, 512 * k,
                               [[2 * IE, 128], [128 * 2 * IE, 8], [1, IE]])
                usrc = bass.AP(suwT, 512 * k,
                               [[2 * IE, 128], [128 * 2 * IE, 8], [1, IE]])
                dsrc = bass.AP(sdwT, 512 * k * D,
                               [[D, 128], [128 * D, 4], [1, D]])
            g.dma_start(out=wg_buf[wb].ap().rearrange("p (c w) -> p c w", c=8),
                        in_=gsrc).then_inc(SEMS["ldwe%d" % wb], 16)
            g.dma_start(out=wu_buf[wb].ap().rearrange("p (c w) -> p c w", c=8),
                        in_=usrc).then_inc(SEMS["ldwe%d" % wb], 16)
            g.dma_start(out=wd_buf[wb].ap().rearrange("p (c w) -> p c w", c=4),
                        in_=dsrc).then_inc(SEMS["ldwe%d" % wb], 16)

        sch.emit("gpsimd", wload)
        return sch.inc("ldwe%d" % wb, 48)

    wld = [0, 0]
    if stop_after >= 4:
        wld[0] = emit_wload(0, 0)

    acc_tok = {}
    wbuf_pe = [0, 0]
    psg_war4 = [0, 0]
    psu_war4 = [0, 0]
    psd_war4 = [0, 0]
    h_pe = 0
    sg_dve4 = [0, 0]

    for eidx in range(10 if stop_after >= 4 else 0):
        wb = eidx % 2
        if eidx < 9:
            sch.wait("gpsimd", "pe", wbuf_pe[1 - wb])
            wld[1 - wb] = emit_wload(eidx + 1, 1 - wb)
        sch.wait("tensor", "ldwe%d" % wb, wld[wb])

        for tb in range(T // 512):
            t_h_last = 0
            for ic in range(4):
                gb = ic % 2
                sch.wait("tensor", "act", psg_war4[gb])
                sch.wait("tensor", "dve", psu_war4[gb])
                sch.wait("tensor", "pe", h_pe)

                def gumm(t, tb=tb, ic=ic, gb=gb, wb=wb):
                    for dc in range(8):
                        t.matmul(psGx[gb][:, :],
                                 wg_buf[wb][:, dc * IE + ic * 128:
                                            dc * IE + ic * 128 + 128],
                                 xn_own[:, dc * T + tb * 512:
                                        dc * T + tb * 512 + 512],
                                 start=(dc == 0), stop=(dc == 7))
                    for dc in range(8):
                        ins = t.matmul(
                            psUx[gb][:, :],
                            wu_buf[wb][:, dc * IE + ic * 128:
                                       dc * IE + ic * 128 + 128],
                            xn_own[:, dc * T + tb * 512:
                                   dc * T + tb * 512 + 512],
                            start=(dc == 0), stop=(dc == 7))
                    ins.then_inc(SEMS["pe"], 1)

                sch.emit("tensor", gumm)
                t_gu = sch.inc("pe")
                wbuf_pe[wb] = max(wbuf_pe[wb], t_gu)

                sch.wait("scalar", "pe", t_gu)
                sch.wait("scalar", "dve", sg_dve4[gb])
                sch.emit("scalar", lambda s, gb=gb: s.activation(
                    sg_buf[gb][:, :], psGx[gb][:, :], AF.Silu)
                    .then_inc(SEMS["act"], 1))
                t_silu = sch.inc("act")
                psg_war4[gb] = t_silu

                sch.wait("vector", "act", t_silu)
                sch.wait("vector", "pe", h_pe)
                sch.emit("vector", lambda v, gb=gb, ic=ic: v.tensor_tensor(
                    h_sb[:, ic * 512:(ic + 1) * 512], psUx[gb][:, :],
                    sg_buf[gb][:, :], ALU.mult).then_inc(SEMS["dve"], 1))
                t_h = sch.inc("dve")
                psu_war4[gb] = t_h
                sg_dve4[gb] = t_h
                t_h_last = t_h

            down_last = 0
            for j in range(4):
                tt = tb * 4 + j
                for hc in range(2):
                    db = (j * 2 + hc) % 2
                    sch.wait("tensor", "dve", t_h_last)
                    sch.wait("tensor", "dve", psd_war4[db])

                    def dmm(t, j=j, hc=hc, db=db, wb=wb):
                        for ic in range(4):
                            ins = t.matmul(
                                psD[db][:, :],
                                h_sb[:, ic * 512 + j * 128:
                                     ic * 512 + j * 128 + 128],
                                wd_buf[wb][:, ic * D + hc * 512:
                                           ic * D + hc * 512 + 512],
                                start=(ic == 0), stop=(ic == 3))
                        ins.then_inc(SEMS["pe"], 1)

                    sch.emit("tensor", dmm)
                    t_d = sch.inc("pe")
                    down_last = t_d
                    wbuf_pe[wb] = max(wbuf_pe[wb], t_d)

                    sch.wait("vector", "pe", t_d)
                    if eidx == 0:
                        sch.wait("vector", "ldxo", xo_ld)
                        sch.wait("vector", "dve", t_coef)
                    sch.wait("vector", "dve", acc_tok.get((tt, hc), 0))

                    if eidx < 8:
                        def accf(v, tt=tt, hc=hc, db=db, eidx=eidx):
                            v.scalar_tensor_tensor(
                                out_acc[:, tt * D + hc * 512:
                                        tt * D + hc * 512 + 512],
                                psD[db][:, :],
                                coef_sb[:, tt * NE + eidx: tt * NE + eidx + 1],
                                out_acc[:, tt * D + hc * 512:
                                        tt * D + hc * 512 + 512],
                                ALU.mult, ALU.add).then_inc(SEMS["dve"], 1)
                    else:
                        def accf(v, tt=tt, hc=hc, db=db):
                            v.tensor_tensor(
                                out_acc[:, tt * D + hc * 512:
                                        tt * D + hc * 512 + 512],
                                psD[db][:, :],
                                out_acc[:, tt * D + hc * 512:
                                        tt * D + hc * 512 + 512],
                                ALU.add).then_inc(SEMS["dve"], 1)

                    sch.emit("vector", accf)
                    tk3 = sch.inc("dve")
                    acc_tok[(tt, hc)] = tk3
                    psd_war4[db] = tk3
            h_pe = down_last

    es_p4.close()
    ps_p4.close()
    sch.barrier()

    es_p5, ps_p5 = ExitStack(), ExitStack()
    ctxT = sb(es_p5, "ctxT", [128, 8 * T], BF16)
    wo_sb = sb(es_p5, "wo_sb", [128, 8 * D], BF16)
    bo_sb = sb(es_p5, "bo_sb", [1, D], BF16)
    psT = [ps(ps_p5, f"psT{i}", [128, 128], BF16) for i in range(2)]
    psO = [ps(ps_p5, f"psO{i}", [128, 512]) for i in range(2)]

    def p5w(g):
        g.dma_start(out=wo_sb.ap().rearrange("p (c w) -> p c w", c=8),
                    in_=bass.AP(wo, 0, [[D, 128], [128 * D, 8], [1, D]]))\
            .then_inc(SEMS["ldwo"], 16)
        g.dma_start(out=bo_sb[:, :], in_=bo[:, :]).then_inc(SEMS["ldwo"], 16)

    sch.emit("gpsimd", p5w)
    t_wo = sch.inc("ldwo", 32)

    psT_war = [0, 0]
    for tt in range(NQT if stop_after >= 5 else 0):
        for dc in range(8):
            pb2 = dc % 2
            sch.wait("tensor", "dve", max(t_consts, psT_war[pb2]))
            sch.emit("tensor", lambda t, tt=tt, dc=dc, pb2=pb2: t.transpose(
                psT[pb2][:, :],
                ctx_tm[:, tt * D + dc * 128: tt * D + dc * 128 + 128],
                ident[:, :]).then_inc(SEMS["pe"], 1))
            t_tr = sch.inc("pe")
            sch.wait("vector", "pe", t_tr)
            sch.emit("vector", lambda v, tt=tt, dc=dc, pb2=pb2: v.tensor_copy(
                ctxT[:, dc * T + tt * 128: dc * T + tt * 128 + 128],
                psT[pb2][:, :]).then_inc(SEMS["dve"], 1))
            psT_war[pb2] = sch.inc("dve")
    t_trans_all = sch.now("dve")

    psO_war = [0, 0]
    y_st = 0
    sch.wait("tensor", "ldwo", t_wo)
    for tt in range(NQT if stop_after >= 5 else 0):
        for hc in range(2):
            ob = (tt * 2 + hc) % 2
            sch.wait("tensor", "dve", t_trans_all)
            sch.wait("tensor", "dve", psO_war[ob])

            def womm(t, tt=tt, hc=hc, ob=ob):
                for dc in range(8):
                    t.matmul(psO[ob][:, :],
                             ctxT[:, dc * T + tt * 128: dc * T + tt * 128 + 128],
                             wo_sb[:, dc * D + hc * 512:
                                   dc * D + hc * 512 + 512],
                             start=(dc == 0), stop=False)
                t.matmul(psO[ob][:, :], ones_row_bf[:, :],
                         bo_sb[:, hc * 512: hc * 512 + 512], start=False,
                         stop=True).then_inc(SEMS["pe"], 1)

            sch.emit("tensor", womm)
            t_wmm2 = sch.inc("pe")

            sch.wait("vector", "pe", t_wmm2)
            sch.wait("vector", "dve", acc_tok.get((tt, hc), 0))
            sch.emit("vector", lambda v, tt=tt, hc=hc, ob=ob: v.tensor_tensor(
                out_acc[:, tt * D + hc * 512: tt * D + hc * 512 + 512],
                psO[ob][:, :],
                out_acc[:, tt * D + hc * 512: tt * D + hc * 512 + 512],
                ALU.add).then_inc(SEMS["dve"], 1))
            tk4 = sch.inc("dve")
            acc_tok[(tt, hc)] = tk4
            psO_war[ob] = tk4

        sch.wait("sync", "dve", max(acc_tok[(tt, 0)], acc_tok[(tt, 1)]))
        sch.emit("sync", lambda sy, tt=tt: sy.dma_start(
            out=y[tt * 128:(tt + 1) * 128, :],
            in_=out_acc[:, tt * D:(tt + 1) * D]).then_inc(SEMS["st"], 16))
        y_st = sch.inc("st", 16)

    if debug:
        def dbgdump(sy):
            sy.dma_start(out=dbg_ctx[:, :], in_=ctx_tm[:, :])\
                .then_inc(SEMS["st"], 16)
            sy.dma_start(out=dbg_xn[:, :], in_=xn_own[:, :])\
                .then_inc(SEMS["st"], 16)
            sy.dma_start(out=dbg_sc[:, :], in_=sc_row[:, :])\
                .then_inc(SEMS["st"], 16)

        sch.emit("sync", dbgdump)
        y_st = sch.inc("st", 48)

    sch.wait("sync", "st", y_st)
    for s_ in SEM_NAMES:
        sch.wait("sync", s_, sch.now(s_))

    es_p5.close()
    ps_p5.close()
    es_tail.close()

    sems_ctx = ExitStack()
    for name in SEM_NAMES:
        SEMS[name] = sems_ctx.enter_context(nc.semaphore(name))

    with es_root, sems_ctx, nc.Block() as block:
        @block.tensor
        def _(t):
            for fn in sch.steps["tensor"]:
                fn(t)

        @block.vector
        def _(v):
            for fn in sch.steps["vector"]:
                fn(v)

        @block.scalar
        def _(s):
            for fn in sch.steps["scalar"]:
                fn(s)

        @block.gpsimd
        def _(g):
            for fn in sch.steps["gpsimd"]:
                fn(g)

        @block.sync
        def _(sy):
            for fn in sch.steps["sync"]:
                fn(sy)

    nc._ledger = dict(sch.cnt)
    nc.compile()
    return nc


def make_tbl(c, S):
    CH = S // 8
    words = []
    for hp in range(8):
        for i in range(CH // 512):
            for chunk, kt, d in slot_list(c, hp, i, CH):
                words.append(kt * 2 + chunk)
    w = np.asarray(words, np.uint32)
    assert w.max() < 256 and len(w) % 4 == 0
    packed = (w[0::4] | (w[1::4] << 8) | (w[2::4] << 16) | (w[3::4] << 24))
    return packed.astype(np.uint32).view(np.int32).reshape(1, -1)


def make_in_maps(inputs):
    x = np.asarray(inputs["L0_attn_x"], np.float32)
    B, S, _ = x.shape
    gw = np.asarray(inputs["L0_ffn_gate_w"], np.float32)
    egwT = np.ascontiguousarray(
        np.asarray(inputs["L0_ffn_expert_gate_w"], np.float32).transpose(0, 2, 1))
    euwT = np.ascontiguousarray(
        np.asarray(inputs["L0_ffn_expert_up_w"], np.float32).transpose(0, 2, 1))
    edwT = np.ascontiguousarray(
        np.asarray(inputs["L0_ffn_expert_down_w"], np.float32).transpose(0, 2, 1))
    sgwT = np.ascontiguousarray(
        np.asarray(inputs["L0_ffn_shared_gate_w"], np.float32).T)
    suwT = np.ascontiguousarray(
        np.asarray(inputs["L0_ffn_shared_up_w"], np.float32).T)
    sdwT = np.ascontiguousarray(
        np.asarray(inputs["L0_ffn_shared_down_w"], np.float32).T)
    wv_heads = np.asarray(inputs["L0_attn_Wv"], np.float32)
    wv_aug = np.zeros((D, NH * 65), np.float32)
    for h in range(NH):
        wv_aug[:, h * 65: h * 65 + 64] = wv_heads[:, h * 64:(h + 1) * 64]
    shared = {
        "wq": np.ascontiguousarray(np.asarray(inputs["L0_attn_Wq"], np.float32)),
        "wk": np.ascontiguousarray(np.asarray(inputs["L0_attn_Wk"], np.float32)),
        "wv": wv_aug,
        "wo": np.ascontiguousarray(np.asarray(inputs["L0_attn_Wo"], np.float32)),
        "bo": np.asarray(inputs["L0_attn_bo"], np.float32).reshape(1, D),
        "gwT": np.ascontiguousarray(gw.T),
        "lb": np.asarray(inputs["L0_ffn_lb_bias"], np.float32).reshape(1, NE),
        "egwT": egwT, "euwT": euwT, "edwT": edwT,
        "sgwT": sgwT, "suwT": suwT, "sdwT": sdwT,
    }
    maps = []
    CH = S // 8
    for c in range(8):
        own = np.concatenate([x[0, c * CH:(c + 1) * CH, :],
                              x[1, (7 - c) * CH:(8 - c) * CH, :]], axis=0)
        rest = []
        for b in range(2):
            for p in range(8):
                if (b == 0 and p == c) or (b == 1 and p == 7 - c):
                    continue
                rest.append(x[b, p * CH:(p + 1) * CH, :])
        x_perm = np.concatenate([own] + rest, axis=0)
        x_pfx = np.concatenate([x[0, : CH * (c + 1), :],
                                x[1, : CH * (8 - c), :]], axis=0)
        m = dict(shared)
        m["x_fm"] = np.ascontiguousarray(x_perm.T)
        m["x_pfx"] = np.ascontiguousarray(x_pfx.T)
        m["x_own"] = np.ascontiguousarray(own)
        m["tbl"] = make_tbl(c, S)
        maps.append(m)
    return maps


def kernel(**inputs):
    x = np.asarray(inputs["L0_attn_x"])
    B, S, _ = x.shape
    nc = build_nc(S)
    in_maps = make_in_maps(inputs)
    res = run_bass_kernel_spmd(nc, in_maps, core_ids=list(range(8)))
    CH = S // 8
    out = np.zeros((2, S, D), np.float32)
    for c in range(8):
        yv = res.results[c]["y"]
        out[0, c * CH:(c + 1) * CH, :] = yv[0:CH]
        out[1, (7 - c) * CH:(8 - c) * CH, :] = yv[CH:2 * CH]
    return out
